# version 23
# baseline (speedup 1.0000x reference)
import numpy as np
import ml_dtypes
from contextlib import ExitStack

import concourse.bass as bass
import concourse.mybir as mybir
from concourse.bass_utils import run_bass_kernel_spmd

F32 = mybir.dt.float32
BF16 = mybir.dt.bfloat16
I32 = mybir.dt.int32
AF = mybir.ActivationFunctionType
AX = mybir.AxisListType
ALU = mybir.AluOpType

FULL_CFG = dict(D=1024, NH=16, DH=64, HID=3072, T=4096, B=4,
                SCALE=0.125, QSCALE=1.0 / 127.0, EPS=1e-5)


def _own_blocks(cfg, parity):
    return [2 * j + parity for j in range(cfg["T"] // 256)]


def preload_table_set(nc, set_name):
    from concourse.hw_specs import get_activation_tables
    names = list(get_activation_tables(nc.m.arch))
    idx = names.index(set_name)
    return nc.scalar.add_instruction(mybir.InstLoadActFuncSet(
        name=nc.get_next_instruction_name(), act_func_set_id=idx, ins=[], outs=[]))


class Sem:
    def __init__(self, stack, nc, name):
        self.h = stack.enter_context(nc.semaphore(name))
        self.n = 0

    def inc(self, instr, k=1):
        instr.then_inc(self.h, k)
        self.n += k
        return instr


def build_program(cfg):
    D, NH, DH, HID, T = cfg["D"], cfg["NH"], cfg["DH"], cfg["HID"], cfg["T"]
    NCH = D // 128
    NHC = HID // 128
    NB = T // 128
    TOWN = T // 2
    NBANK = TOWN // 512
    NKVS = T // 512
    NQS = TOWN // 512
    NPASS = 4
    HPP = NH // (2 * NPASS)
    PD = D // NPASS
    FFS = min(1024, TOWN)
    NFS = TOWN // FFS
    NH2 = FFS // 512
    VGW = 2 * HPP * 65

    nc = bass.Bass("TRN2", target_bir_lowering=False, num_devices=1)

    xfm = nc.dram_tensor("xfm", [D, T], F32, kind="ExternalInput").ap()
    xq = nc.dram_tensor("xq", [D, TOWN], F32, kind="ExternalInput").ap()
    wqT = nc.dram_tensor("wqT", [D, D], F32, kind="ExternalInput").ap()
    wkT = nc.dram_tensor("wkT", [D, D], F32, kind="ExternalInput").ap()
    wvT = nc.dram_tensor("wvT", [D, D], F32, kind="ExternalInput").ap()
    woT = nc.dram_tensor("woT", [D, D], F32, kind="ExternalInput").ap()
    wgT = nc.dram_tensor("wgT", [D, HID], I32, kind="ExternalInput").ap()
    wuT = nc.dram_tensor("wuT", [D, HID], I32, kind="ExternalInput").ap()
    wdT = nc.dram_tensor("wdT", [HID, D], I32, kind="ExternalInput").ap()
    maskA = nc.dram_tensor("maskA", [128, 128], BF16, kind="ExternalInput").ap()
    maskB = nc.dram_tensor("maskB", [128, 128], BF16, kind="ExternalInput").ap()
    ematd = nc.dram_tensor("ematd", [2, 128], F32, kind="ExternalInput").ap()
    yout = nc.dram_tensor("yout", [D, TOWN], F32, kind="ExternalOutput").ap()
    ldram = nc.dram_tensor("ldram", [NH, TOWN], F32, kind="Internal").ap()
    linvd = nc.dram_tensor("linvd", [NH, TOWN], F32, kind="Internal").ap()
    debug = cfg.get("debug", False)
    if debug:
        dbg_o = nc.dram_tensor("dbg_o", [128, NCH * TOWN], BF16, kind="ExternalOutput").ap()
        dbg_y1 = nc.dram_tensor("dbg_y1", [128, NCH * TOWN], F32, kind="ExternalOutput").ap()

    st = ExitStack()
    with st:
        consts = st.enter_context(nc.sbuf_tensor("consts", [128, 4], F32))
        ones_col_bf = st.enter_context(nc.sbuf_tensor("ones_col_bf", [128, 1], BF16))
        ones_row_f = st.enter_context(nc.sbuf_tensor("ones_row_f", [1, 128], F32))
        emat = st.enter_context(nc.sbuf_tensor("emat", [2, 128], F32))
        mka = st.enter_context(nc.sbuf_tensor("mka", [128, 128], BF16))
        mkb = st.enter_context(nc.sbuf_tensor("mkb", [128, 128], BF16))
        ofm_st = ExitStack()
        ofm = ofm_st.enter_context(nc.sbuf_tensor("ofm", [128, NCH * TOWN], BF16,
                                                  side="right"))

        s_xst = [Sem(st, nc, f"s_xst{i}") for i in range(2)]
        s_wld = [Sem(st, nc, f"s_wld{i}") for i in range(2)]
        s_wo = Sem(st, nc, "s_wo")
        s_ini = Sem(st, nc, "s_ini")
        s_prj = Sem(st, nc, "s_prj")
        s_evV = Sem(st, nc, "s_evV")
        s_s = Sem(st, nc, "s_s")
        s_exp = Sem(st, nc, "s_exp")
        s_msk = Sem(st, nc, "s_msk")
        s_pv = Sem(st, nc, "s_pv")
        s_oev = Sem(st, nc, "s_oev")
        s_ldm = [Sem(st, nc, f"s_ldm{i}") for i in range(4)]
        s_lpr = Sem(st, nc, "s_lpr")
        s_lwb = Sem(st, nc, "s_lwb")
        s_lp = [Sem(st, nc, f"s_lp{i}") for i in range(8)]
        s_cb = Sem(st, nc, "s_cb")
        s_nrm = Sem(st, nc, "s_nrm")
        s_res = [Sem(st, nc, f"s_res{i}") for i in range(4)]
        s_gu = [Sem(st, nc, f"s_gu{i}") for i in range(2)]
        s_wd = Sem(st, nc, "s_wd")
        s_ffc = Sem(st, nc, "s_ffc")
        s_out = Sem(st, nc, "s_out")
        s_lnA = Sem(st, nc, "s_lnA")
        s_lnV = Sem(st, nc, "s_lnV")
        s_lnP = Sem(st, nc, "s_lnP")

        nc.vector.memset(consts[:, 0:1], -1.0)
        nc.vector.memset(consts[:, 1:2], 0.5)
        nc.vector.memset(consts[:, 2:3], -0.5)
        nc.vector.memset(consts[:, 3:4], float(cfg["EPS"]))
        nc.const_aps.register(F32, -1.0, consts[:, 0:1])
        nc.const_aps.register(F32, 0.5, consts[:, 1:2])
        nc.const_aps.register(F32, -0.5, consts[:, 2:3])
        nc.const_aps.register(F32, float(cfg["EPS"]), consts[:, 3:4])
        nc.vector.memset(ones_col_bf[:, :], 1.0)
        nc.vector.memset(ones_row_f[:, :], 1.0)
        s_ini.inc(nc.gpsimd.dma_start(emat[:, :], ematd), 16)
        s_ini.inc(nc.gpsimd.dma_start(mka[:, :], maskA), 16)
        s_ini.inc(nc.gpsimd.dma_start(mkb[:, :], maskB), 16)
        nc.gpsimd.wait_ge(s_ini.h, s_ini.n)
        preload_table_set(nc, "natural_log_exp_and_others")

        attn_st = ExitStack()
        cbias = [attn_st.enter_context(nc.sbuf_tensor(f"cbias{i}", [128, 1], F32))
                 for i in range(2)]
        qsq2 = [attn_st.enter_context(nc.sbuf_tensor(f"qsq{i}", [128, 512], BF16))
                for i in range(2)]
        mrow = attn_st.enter_context(nc.sbuf_tensor("mrow", [1, 8], F32))
        lstg = [attn_st.enter_context(nc.sbuf_tensor(f"lstg{i}", [1, 512], F32))
                for i in range(4)]
        kbuf = [attn_st.enter_context(nc.sbuf_tensor(f"kbuf{i}", [128, HPP * T], BF16))
                for i in range(2)]
        vbuf = [attn_st.enter_context(nc.sbuf_tensor(f"vbuf{i}", [128, NB * VGW], BF16))
                for i in range(2)]
        qbuf = [attn_st.enter_context(nc.sbuf_tensor(f"qbuf{i}", [128, HPP * TOWN], BF16))
                for i in range(2)]
        xstg = [attn_st.enter_context(nc.sbuf_tensor(f"xstg{i}", [128, NCH * 512], BF16))
                for i in range(2)]
        wqb = [attn_st.enter_context(nc.sbuf_tensor(f"wqb{i}", [128, NCH * PD], BF16))
               for i in range(2)]
        wkb = [attn_st.enter_context(nc.sbuf_tensor(f"wkb{i}", [128, NCH * PD], BF16))
               for i in range(2)]
        wvb = [attn_st.enter_context(nc.sbuf_tensor(f"wvb{i}", [128, NCH * PD], BF16))
               for i in range(2)]
        pt = [attn_st.enter_context(nc.sbuf_tensor(f"pt{i}", [128, 1024], BF16))
              for i in range(2)]
        LRF = 4 * TOWN // 32
        lrecA = attn_st.enter_context(nc.sbuf_tensor("lrecA", [32, LRF], F32))
        lrecB = attn_st.enter_context(nc.sbuf_tensor("lrecB", [32, LRF], F32))
        lpair = [attn_st.enter_context(nc.sbuf_tensor(f"lpair{i}", [2, 512], F32))
                 for i in range(8)]
        for i in range(2):
            vt4 = vbuf[i][:, :].rearrange("p (jb h c) -> p jb h c", jb=NB, h=2 * HPP)
            nc.vector.memset(vt4[:, :, :, 64:65], 1.0)

        att_ps = ExitStack()
        sp = [att_ps.enter_context(nc.psum_tensor(f"sp{i}", [128, 1024], F32))
              for i in range(2)]
        ops = [att_ps.enter_context(nc.psum_tensor(f"op{i}", [65, 512], F32))
               for i in range(2)]
        pp = [att_ps.enter_context(nc.psum_tensor(f"pp{i}", [128, 512], F32))
              for i in range(2)]

        nc.all_engine_barrier()

        slot_free = [0, 0]
        pp_free = [0, 0]
        gidx = [0]

        def stage_x(src_ap, slot, t0):
            dst = xstg[slot][:, :].rearrange("p (c t) -> p c t", c=NCH)
            src = src_ap.rearrange("(c p) t -> p c t", p=128)[:, :, t0:t0 + 512]
            return nc.gpsimd.dma_start(dst, src)

        def load_w_quarter(dst, wsrc, p, sem):
            d = dst[:, :].rearrange("p (c n) -> p c n", c=NCH)
            s = wsrc.rearrange("(c p) n -> p c n", p=128)[:, :, p * PD:(p + 1) * PD]
            sem.inc(nc.gpsimd.dma_start(d, s), 16)

        def proj_gen(p):
            pb = p % 2
            load_w_quarter(wqb[pb], wqT, p, s_wld[pb])
            load_w_quarter(wkb[pb], wkT, p, s_wld[pb])
            load_w_quarter(wvb[pb], wvT, p, s_wld[pb])
            w_ready = s_wld[pb].n
            state = {"first": True}

            def group(emit_mm, emit_evac):
                pi = gidx[0] % 2
                gidx[0] += 1
                for m in range(NCH):
                    if m == 0:
                        nc.tensor.wait_ge(s_evV.h, pp_free[pi])
                        if state["first"]:
                            nc.tensor.wait_ge(s_wld[pb].h, w_ready)
                            state["first"] = False
                    last = emit_mm(pp[pi], m)
                    if m == NCH - 1:
                        s_prj.inc(last)
                        emit_evac(pp[pi], s_prj.n)
                        pp_free[pi] = s_evV.n
                    yield

            for ts in range(NKVS):
                slot = ts % 2
                nc.gpsimd.wait_ge(s_prj.h, slot_free[slot])
                s_xst[slot].inc(stage_x(xfm, slot, ts * 512), 16)
                x_ready = s_xst[slot].n
                nc.tensor.wait_ge(s_xst[slot].h, x_ready)
                xs = xstg[slot]
                for nk in range(HPP):
                    def mm_k(ps, m, nk=nk, xs=xs):
                        return nc.tensor.matmul(
                            ps[:, 0:512],
                            wkb[pb][:, m * PD + nk * 128: m * PD + nk * 128 + 128],
                            xs[:, m * 512:(m + 1) * 512],
                            start=(m == 0), stop=(m == NCH - 1))

                    def ev_k(ps, tgt, nk=nk, ts=ts):
                        nc.vector.wait_ge(s_prj.h, tgt)
                        s_evV.inc(nc.vector.tensor_copy(
                            kbuf[pb][:, nk * T + ts * 512: nk * T + ts * 512 + 512],
                            ps[:, 0:512]))
                    yield from group(mm_k, ev_k)
                for jb in range(4):
                    def mm_v(ps, m, jb=jb, xs=xs):
                        return nc.tensor.matmul(
                            ps[:, 0:PD],
                            xs[:, m * 512 + jb * 128: m * 512 + jb * 128 + 128],
                            wvb[pb][:, m * PD:(m + 1) * PD],
                            start=(m == 0), stop=(m == NCH - 1))

                    def ev_v(ps, tgt, jb=jb, ts=ts):
                        nc.vector.wait_ge(s_prj.h, tgt)
                        g0 = (ts * 4 + jb) * VGW
                        dst3 = vbuf[pb][:, g0:g0 + VGW].rearrange(
                            "p (h c) -> p h c", h=2 * HPP)[:, :, 0:64]
                        src3 = ps[:, 0:PD].rearrange("p (h c) -> p h c", h=2 * HPP)
                        s_evV.inc(nc.vector.tensor_copy(dst3, src3))
                    yield from group(mm_v, ev_v)
                slot_free[slot] = s_prj.n
            for qs in range(NQS):
                slot = qs % 2
                nc.gpsimd.wait_ge(s_prj.h, slot_free[slot])
                s_xst[slot].inc(stage_x(xq, slot, qs * 512), 16)
                x_ready = s_xst[slot].n
                nc.tensor.wait_ge(s_xst[slot].h, x_ready)
                xs = xstg[slot]
                for nq in range(HPP):
                    def mm_q(ps, m, nq=nq, xs=xs):
                        return nc.tensor.matmul(
                            ps[:, 0:512],
                            wqb[pb][:, m * PD + nq * 128: m * PD + nq * 128 + 128],
                            xs[:, m * 512:(m + 1) * 512],
                            start=(m == 0), stop=(m == NCH - 1))

                    def ev_q(ps, tgt, nq=nq, qs=qs):
                        nc.vector.wait_ge(s_prj.h, tgt)
                        s_evV.inc(nc.vector.tensor_scalar_mul(
                            qbuf[pb][:, nq * TOWN + qs * 512: nq * TOWN + qs * 512 + 512],
                            ps[:, 0:512], cfg["SCALE"]))
                    yield from group(mm_q, ev_q)
                slot_free[slot] = s_prj.n

        def c_compute(p):
            pb = p % 2
            cbp = pp[0][:, 0:1]
            nc.tensor.wait_ge(s_evV.h, s_evV.n)
            nc.scalar.wait_ge(s_evV.h, s_evV.n)
            nc.vector.wait_ge(s_nrm.h, s_nrm.n)
            nc.vector.wait_ge(s_lnP.h, s_lnP.n)
            s_lnV.inc(nc.vector.memset(mrow[:, 0:2], 0.0))

            srcs = ([(kbuf[pb][:, ck * T + ts * 512: ck * T + ts * 512 + 512], 0)
                     for ts in range(NKVS) for ck in range(HPP)] +
                    [(qbuf[pb][:, cq * TOWN + qs * 512: cq * TOWN + qs * 512 + 512], 1)
                     for qs in range(NQS) for cq in range(HPP)])
            mm_of = [0, 0]
            red_of = [0, 0]
            for i, (src_ap, slot) in enumerate(srcs):
                bi = i % 2
                nrmp = pp[bi][0:1, 0:512]
                nc.scalar.wait_ge(s_lnP.h, mm_of[bi])
                s_lnA.inc(nc.scalar.activation(qsq2[bi][:, :], src_ap, AF.Square))
                nc.tensor.wait_ge(s_lnA.h, s_lnA.n)
                nc.tensor.wait_ge(s_lnV.h, red_of[bi])
                s_lnP.inc(nc.tensor.matmul(nrmp, ones_col_bf[:, :], qsq2[bi][:, :],
                                           start=True, stop=True))
                mm_of[bi] = s_lnP.n
                nc.vector.wait_ge(s_lnP.h, mm_of[bi])
                s_lnV.inc(nc.vector.reduce_max(mrow[:, 2 + bi:3 + bi], nrmp, axis=AX.X))
                red_of[bi] = s_lnV.n
                nc.vector.wait_ge(s_lnV.h, s_lnV.n)
                s_lnV.inc(nc.vector.tensor_max(mrow[:, slot:slot + 1],
                                               mrow[:, slot:slot + 1],
                                               mrow[:, 2 + bi:3 + bi]))
            nc.scalar.wait_ge(s_lnV.h, s_lnV.n)
            s_lnA.inc(nc.scalar.activation(mrow[:, 4:5], mrow[:, 0:1], AF.Ln))
            nc.scalar.wait_ge(s_lnA.h, s_lnA.n)
            s_lnA.inc(nc.scalar.activation(mrow[:, 5:6], mrow[:, 1:2], AF.Ln))
            nc.vector.wait_ge(s_lnA.h, s_lnA.n)
            s_lnV.inc(nc.vector.tensor_add(mrow[:, 4:5], mrow[:, 4:5], mrow[:, 5:6]))
            nc.scalar.wait_ge(s_lnV.h, s_lnV.n)
            s_lnA.inc(nc.scalar.activation(mrow[:, 6:7], mrow[:, 4:5], AF.Exp, scale=0.5))
            nc.vector.wait_ge(s_lnA.h, s_lnA.n)
            s_lnV.inc(nc.vector.tensor_scalar_mul(mrow[:, 6:7], mrow[:, 6:7], -1.0))
            nc.tensor.wait_ge(s_lnV.h, s_lnV.n)
            s_lnP.inc(nc.tensor.matmul(cbp, ones_row_f[:, :], mrow[:, 6:7],
                                       start=True, stop=True))
            nc.vector.wait_ge(s_lnP.h, s_lnP.n)
            s_cb.inc(nc.vector.tensor_copy(cbias[pb][:, :], cbp))
            nc.tensor.wait_ge(s_cb.h, s_cb.n)

        lp_slot = [0]

        def norm_gen(p):
            for i in range(4):
                nc.gpsimd.wait_ge(s_ldm[i].h, s_ldm[i].n)
            s_lpr.inc(nc.gpsimd.dma_start(
                lrecA[:, :], ldram[4 * p:4 * p + 4].rearrange("h (a f) -> (h a) f", f=LRF)), 16)
            lrd = s_lpr.n

            def gen():
                nc.vector.wait_ge(s_lpr.h, lrd)
                s_nrm.inc(nc.vector.reciprocal(lrecB[:, :], lrecA[:, :]))
                nc.gpsimd.wait_ge(s_nrm.h, s_nrm.n)
                s_lwb.inc(nc.gpsimd.dma_start(
                    linvd[4 * p:4 * p + 4].rearrange("h (a f) -> (h a) f", f=LRF),
                    lrecB[:, :]), 16)
                wb = s_lwb.n
                yield
                idxs = [(2 * p + g, s) for g in range(2) for s in range(NQS)]
                ldone = {}
                def issue(i):
                    gcx, s = idxs[i]
                    j = lp_slot[0]
                    lp_slot[0] = (j + 1) % 8
                    nc.gpsimd.wait_ge(s_lwb.h, wb)
                    nc.gpsimd.wait_ge(s_evV.h, ldone.get(("f", j), 0))
                    s_lp[j].inc(nc.gpsimd.dma_start(
                        lpair[j][:, :],
                        linvd[2 * gcx:2 * gcx + 2, s * 512:s * 512 + 512]), 16)
                    ldone[i] = (j, s_lp[j].n)
                for i in range(len(idxs)):
                    issue(i)
                yield
                for i, (gcx, s) in enumerate(idxs):
                    j, lv = ldone[i]
                    pi = gidx[0] % 2
                    gidx[0] += 1
                    nc.tensor.wait_ge(s_evV.h, pp_free[pi])
                    nc.tensor.wait_ge(s_lp[j].h, lv)
                    s_prj.inc(nc.tensor.matmul(pp[pi][:, :], emat[:, :], lpair[j][:, :],
                                               start=True, stop=True))
                    tgt = s_prj.n
                    nc.vector.wait_ge(s_prj.h, tgt)
                    q0 = s * 512
                    s_evV.inc(nc.vector.tensor_mul(
                        ofm[:, gcx * TOWN + q0: gcx * TOWN + q0 + 512],
                        ofm[:, gcx * TOWN + q0: gcx * TOWN + q0 + 512], pp[pi][:, :]))
                    pp_free[pi] = s_evV.n
                    ldone[("f", j)] = s_evV.n
                    yield
            return gen()

        def chaing(*gens):
            for g in gens:
                yield from g

        exp_of_buf = [0, 0]
        pv_of_buf = [0, 0]
        oev_last = [0]
        lslot = [0]

        def attention_pass(p, filler, rate):
            pb = p % 2
            first_exp = [True]

            def emit_s(hp, bank, kb):
                d = kb - 8 * bank
                js = d // 2 if d > 0 else 0
                buf = kb % 2
                qbase = bank * 512
                nc.tensor.wait_ge(s_exp.h, exp_of_buf[buf])
                nc.tensor.matmul(
                    sp[buf][:, js * 128: 512],
                    kbuf[pb][0:64, hp * T + kb * 128: hp * T + kb * 128 + 128],
                    qbuf[pb][0:64, hp * TOWN + qbase + js * 128: hp * TOWN + qbase + 512],
                    start=True, stop=True)
                s_s.inc(nc.tensor.matmul(
                    sp[buf][:, 512 + js * 128: 1024],
                    kbuf[pb][64:128, hp * T + kb * 128: hp * T + kb * 128 + 128],
                    qbuf[pb][64:128, hp * TOWN + qbase + js * 128: hp * TOWN + qbase + 512],
                    start=True, stop=True))
                nc.scalar.wait_ge(s_s.h, s_s.n)
                nc.scalar.wait_ge(s_pv.h, pv_of_buf[buf])
                if first_exp[0]:
                    nc.scalar.wait_ge(s_cb.h, s_cb.n)
                    first_exp[0] = False
                sview = sp[buf][:, :].rearrange("p (h q) -> p h q", h=2)[:, :, js * 128:512]
                pview = pt[buf][:, :].rearrange("p (h q) -> p h q", h=2)[:, :, js * 128:512]
                s_exp.inc(nc.scalar.activation(pview, sview, AF.Exp,
                                               bias=cbias[pb][:, 0:1]))
                exp_of_buf[buf] = s_exp.n
                if d >= 0:
                    mk = mka if d % 2 == 0 else mkb
                    nc.vector.wait_ge(s_exp.h, s_exp.n)
                    nc.vector.tensor_mul(
                        pt[buf][:, js * 128: js * 128 + 128],
                        pt[buf][:, js * 128: js * 128 + 128], mk[:, :])
                    s_msk.inc(nc.vector.tensor_mul(
                        pt[buf][:, 512 + js * 128: 512 + js * 128 + 128],
                        pt[buf][:, 512 + js * 128: 512 + js * 128 + 128], mk[:, :]))
                return (s_exp.n, s_msk.n if d >= 0 else None)

            def emit_pv(hp, bank, kb, deps, o_started):
                d = kb - 8 * bank
                js = d // 2 if d > 0 else 0
                buf = kb % 2
                nkb = 8 * bank + 8
                exp_n, msk_n = deps
                if msk_n is not None:
                    nc.tensor.wait_ge(s_msk.h, msk_n)
                else:
                    nc.tensor.wait_ge(s_exp.h, exp_n)
                if not o_started:
                    nc.tensor.wait_ge(s_oev.h, oev_last[0])
                lastpv = None
                for hl in range(2):
                    lastpv = nc.tensor.matmul(
                        ops[hl][:, js * 128: 512],
                        vbuf[pb][:, kb * VGW + (2 * hp + hl) * 65:
                                 kb * VGW + (2 * hp + hl) * 65 + 65],
                        pt[buf][:, hl * 512 + js * 128: hl * 512 + 512],
                        start=(not o_started), stop=(kb == nkb - 1),
                        skip_group_check=True)
                s_pv.inc(lastpv)
                pv_of_buf[buf] = s_pv.n

            def fill(n):
                nonlocal filler
                if filler is None:
                    return
                for _ in range(n):
                    if next(filler, StopIteration) is StopIteration:
                        filler = None
                        break

            for hp in range(HPP):
                gc = HPP * p + hp
                for bank in range(NBANK):
                    nkb = 8 * bank + 8
                    qbase = bank * 512
                    deps = emit_s(hp, bank, 0)
                    for kb in range(nkb):
                        if kb + 1 < nkb:
                            ndeps = emit_s(hp, bank, kb + 1)
                        fill(rate)
                        emit_pv(hp, bank, kb, deps, o_started=(kb > 0))
                        if kb + 1 < nkb:
                            deps = ndeps
                    nc.vector.wait_ge(s_pv.h, s_pv.n)
                    for hl in range(2):
                        s_oev.inc(nc.vector.tensor_copy(
                            ofm[hl * 64:(hl + 1) * 64,
                                gc * TOWN + qbase: gc * TOWN + qbase + 512],
                            ops[hl][0:64, :]))
                        ls = lslot[0]
                        nc.vector.wait_ge(s_ldm[ls].h, s_ldm[ls].n)
                        s_oev.inc(nc.vector.tensor_copy(lstg[ls][:, :], ops[hl][64:65, :]))
                        nc.sync.wait_ge(s_oev.h, s_oev.n)
                        g = 2 * gc + hl
                        s_ldm[ls].inc(nc.sync.dma_start(
                            ldram[g:g + 1, qbase:qbase + 512], lstg[ls][:, :]), 16)
                        lslot[0] = (ls + 1) % 4
                    oev_last[0] = s_oev.n
            return filler

        for _ in proj_gen(0):
            pass
        c_compute(0)
        for p in range(NPASS):
            parts = []
            if p + 1 < NPASS:
                parts.append(proj_gen(p + 1))
            if p >= 1:
                parts.append(norm_gen(p - 1))
            filler = chaing(*parts) if parts else None
            filler = attention_pass(p, filler, rate=2)
            if filler is not None:
                for _ in filler:
                    pass
            if p + 1 < NPASS:
                c_compute(p + 1)
        for _ in norm_gen(NPASS - 1):
            pass
        for i in range(4):
            nc.gpsimd.wait_ge(s_ldm[i].h, s_ldm[i].n)
        nc.tensor.wait_ge(s_evV.h, s_evV.n)
        att_ps.close()
        attn_st.close()
        nc.all_engine_barrier()
        if debug:
            s_out.inc(nc.sync.dma_start(dbg_o, ofm[:, :]), 16)
            nc.sync.wait_ge(s_out.h, s_out.n)
            nc.all_engine_barrier()

        y = st.enter_context(nc.sbuf_tensor("y", [128, NCH * TOWN], F32))
        ysq = [st.enter_context(nc.sbuf_tensor(f"ysq{i}", [128, 512], BF16))
               for i in range(8)]
        ybf = [st.enter_context(nc.sbuf_tensor(f"ybf{i}", [128, 512], BF16))
               for i in range(8)]
        rows = st.enter_context(nc.sbuf_tensor("rows", [1, 4 * 512], F32))
        ph_b = ExitStack()
        wob = ph_b.enter_context(nc.sbuf_tensor("wob", [128, NCH * D], BF16))
        xres = [ph_b.enter_context(nc.sbuf_tensor(f"xres{i}", [128, 512], F32))
                for i in range(4)]
        ffn_ps = ExitStack()
        gps = ffn_ps.enter_context(nc.psum_tensor("gps", [128, 512], F32))
        ups = ffn_ps.enter_context(nc.psum_tensor("ups", [128, 512], F32))

        s_woB = Sem(st, nc, "s_woB")
        s_wo.inc(nc.gpsimd.dma_start(
            wob[:, :].rearrange("p (c n) -> p c n", c=NCH)[:, :, 0:D // 2],
            woT.rearrange("(c p) n -> p c n", p=128)[:, :, 0:D // 2]), 16)
        s_woB.inc(nc.gpsimd.dma_start(
            wob[:, :].rearrange("p (c n) -> p c n", c=NCH)[:, :, D // 2:D],
            woT.rearrange("(c p) n -> p c n", p=128)[:, :, D // 2:D]), 16)
        wo_ready = s_wo.n
        woB_ready = s_woB.n

        class LN:

            NRING = 4

            def __init__(self, stp, bc2, ysq, ybf, rows, sA, sV, sP, ready_of=None):
                self.stp, self.bc2 = stp, bc2
                self.ysq, self.ybf, self.rows = ysq, ybf, rows
                self.sA, self.sV, self.sP = sA, sV, sP
                self.nst, self.nbc = len(stp), len(bc2)
                nr = len(ysq)
                self.ysq_free = [0] * nr
                self.sq_done = {}
                self.stats_done = [0] * self.nst
                self.bc_done = [0] * self.nbc
                self.apply_st = [0] * self.nst
                self.apply_bc = [0] * self.nbc
                self.rows_free = 0
                self.ready_of = ready_of

            def stats_act(self, s):
                sA, sP = self.sA, self.sP
                nr = len(self.ysq)
                q0 = s * 512
                if self.ready_of is not None:
                    sem, val = self.ready_of(s)
                    nc.scalar.wait_ge(sem.h, val)
                for m in range(NCH):
                    bi = (s * NCH + m) % nr
                    nc.scalar.wait_ge(sP.h, self.ysq_free[bi])
                    sA.inc(nc.scalar.activation(
                        self.ysq[bi][:, :], y[:, m * TOWN + q0: m * TOWN + q0 + 512],
                        AF.Square))
                    sA.inc(nc.scalar.copy(
                        self.ybf[bi][:, :], y[:, m * TOWN + q0: m * TOWN + q0 + 512]))
                    self.sq_done[(s, m)] = sA.n

            def stats_pe(self, s):
                sV, sP = self.sV, self.sP
                nr = len(self.ysq)
                sb = s % self.nst
                nc.tensor.wait_ge(sV.h, self.apply_st[sb])
                for m in range(NCH):
                    bi = (s * NCH + m) % nr
                    nc.tensor.wait_ge(self.sA.h, self.sq_done.pop((s, m)))
                    nc.tensor.matmul(self.stp[sb][0:1, :], ones_col_bf[:, :],
                                     self.ybf[bi][:, :],
                                     start=(m == 0), stop=(m == NCH - 1))
                    sP.inc(nc.tensor.matmul(
                        self.stp[sb][32:33, :], ones_col_bf[:, :], self.ysq[bi][:, :],
                        start=(m == 0), stop=(m == NCH - 1)))
                    self.ysq_free[bi] = sP.n
                self.stats_done[sb] = sP.n

            def stats(self, s):
                self.stats_act(s)
                self.stats_pe(s)

            def finish_rows(self, s):
                sA, sV, sP = self.sA, self.sV, self.sP
                sb = s % self.nst
                rows = self.rows
                nc.scalar.wait_ge(sP.h, self.stats_done[sb])
                nc.scalar.wait_ge(sP.h, self.rows_free)
                sA.inc(nc.scalar.activation(rows[:, 0:512], self.stp[sb][0:1, :],
                                            AF.Square, scale=1.0 / D))
                nc.vector.wait_ge(sA.h, sA.n)
                nc.vector.wait_ge(sP.h, self.stats_done[sb])
                sV.inc(nc.vector.scalar_tensor_tensor(
                    rows[:, 512:1024], self.stp[sb][32:33, :], 1.0 / D, rows[:, 0:512],
                    op0=ALU.mult, op1=ALU.subtract))
                nc.scalar.wait_ge(sV.h, sV.n)
                sA.inc(nc.scalar.activation(rows[:, 1024:1536], rows[:, 512:1024],
                                            AF.Ln, bias=float(cfg["EPS"])))
                nc.scalar.wait_ge(sA.h, sA.n)
                sA.inc(nc.scalar.activation(rows[:, 1024:1536], rows[:, 1024:1536],
                                            AF.Exp, scale=-0.5))
                nc.vector.wait_ge(sA.h, sA.n)
                sV.inc(nc.vector.scalar_tensor_tensor(
                    rows[:, 1536:2048], self.stp[sb][0:1, :], -1.0 / D,
                    rows[:, 1024:1536], op0=ALU.mult, op1=ALU.mult))
                self.rows_done = sV.n

            def finish_bcast(self, s):
                sA, sV, sP = self.sA, self.sV, self.sP
                sb = s % self.nst
                bb = s % self.nbc
                rows = self.rows
                nc.tensor.wait_ge(sV.h, self.rows_done)
                nc.tensor.wait_ge(sV.h, self.apply_bc[bb])
                nc.tensor.matmul(self.bc2[bb][0][:, :], ones_row_f[:, :],
                                 rows[:, 1024:1536], start=True, stop=True)
                sP.inc(nc.tensor.matmul(self.bc2[bb][1][:, :], ones_row_f[:, :],
                                        rows[:, 1536:2048], start=True, stop=True))
                self.bc_done[bb] = sP.n
                self.rows_free = sP.n

            def apply_m(self, s, m0, m1):
                sV, sP = self.sV, self.sP
                sb = s % self.nst
                bb = s % self.nbc
                q0 = s * 512
                nc.vector.wait_ge(sP.h, self.bc_done[bb])
                for m in range(m0, m1):
                    yv = y[:, m * TOWN + q0: m * TOWN + q0 + 512]
                    sV.inc(nc.vector.tensor_mul(yv, yv, self.bc2[bb][0][:, :]))
                    nc.vector.wait_ge(sV.h, sV.n)
                    sV.inc(nc.vector.tensor_add(yv, yv, self.bc2[bb][1][:, :]))
                if m1 == NCH:
                    self.apply_st[sb] = sV.n
                    self.apply_bc[bb] = sV.n

            def finish_tail(self, s):
                self.finish_bcast(s)
                self.apply_m(s, 0, NCH)

            def finish(self, s):
                self.finish_rows(s)
                self.finish_tail(s)

        ln_ps = ExitStack()
        stp = [ln_ps.enter_context(nc.psum_tensor(f"stp{i}", [33, 512], F32))
               for i in range(2)]
        bc2 = [[ln_ps.enter_context(nc.psum_tensor(f"bc2{i}_{j}", [128, 512], F32))
                for j in range(2)] for i in range(1)]
        wo_ps = ExitStack()
        wps = [wo_ps.enter_context(nc.psum_tensor(f"wps{i}", [128, 512], F32))
               for i in range(2)]
        y_slice_done = [0] * NQS
        ln1_done = [0] * NQS

        ln1 = LN(stp, bc2, ysq, ybf, rows, s_lnA, s_lnV, s_lnP,
                 ready_of=lambda s: (s_evV, y_slice_done[s]))
        wo_iters = [(ncx, s) for s in range(NQS) for ncx in range(NCH)]
        wps_free = [0, 0]
        xr_free = [0, 0, 0, 0]
        xr_done = [0, 0, 0, 0]

        def issue_xres(idx):
            ncx, s = wo_iters[idx]
            j = idx % 4
            sem = s_res[j]
            nc.sync.wait_ge(s_evV.h, xr_free[j])
            sem.inc(nc.sync.dma_start(
                xres[j][:, :],
                xq.rearrange("(c p) t -> p c t", p=128)[:, ncx, s * 512:s * 512 + 512]), 16)
            xr_done[j] = sem.n

        for idx in range(min(4, len(wo_iters))):
            issue_xres(idx)
        for idx, (ncx, s) in enumerate(wo_iters):
            q0 = s * 512
            i = idx % 2
            j = idx % 4
            nc.tensor.wait_ge(s_evV.h, wps_free[i])
            if idx == 0:
                nc.tensor.wait_ge(s_wo.h, wo_ready)
            if idx == NCH // 2:
                nc.tensor.wait_ge(s_woB.h, woB_ready)
            last = None
            for kcx in range(NCH):
                last = nc.tensor.matmul(
                    wps[i][:, :], wob[:, kcx * D + ncx * 128: kcx * D + ncx * 128 + 128],
                    ofm[:, kcx * TOWN + q0: kcx * TOWN + q0 + 512],
                    start=(kcx == 0), stop=(kcx == NCH - 1))
            s_prj.inc(last)
            nc.vector.wait_ge(s_prj.h, s_prj.n)
            nc.vector.wait_ge(s_res[j].h, xr_done[j])
            s_evV.inc(nc.vector.tensor_add(
                y[:, ncx * TOWN + q0: ncx * TOWN + q0 + 512], wps[i][:, :], xres[j][:, :]))
            wps_free[i] = s_evV.n
            xr_free[j] = s_evV.n
            if idx + 4 < len(wo_iters):
                issue_xres(idx + 4)
            if ncx == 0 and s >= 2:
                ln1.apply_m(s - 2, 4, NCH)
                ln1_done[s - 2] = ln1.apply_st[(s - 2) % ln1.nst]
            if ncx == 2 and s >= 1:
                ln1.stats_pe(s - 1)
            if ncx == 5 and s >= 1:
                ln1.finish_rows(s - 1)
            if ncx == 6 and s >= 1:
                ln1.finish_bcast(s - 1)
            if ncx == NCH - 1:
                y_slice_done[s] = s_evV.n
                if s >= 1:
                    ln1.apply_m(s - 1, 0, 4)
                ln1.stats_act(s)
        ln1.apply_m(NQS - 2, 4, NCH)
        ln1_done[NQS - 2] = ln1.apply_st[(NQS - 2) % ln1.nst]
        wo_pe_done = s_prj.n
        wo_dve_done = s_evV.n
        ln1.stats_pe(NQS - 1)
        if NFS < 2:
            ln1.finish(NQS - 1)
            ln1_done[NQS - 1] = ln1.apply_st[(NQS - 1) % ln1.nst]
        ofm_st.close()
        wo_ps.close()
        ph_b.close()
        if debug:
            nc.sync.wait_ge(s_lnV.h, s_lnV.n)
            s_out.inc(nc.sync.dma_start(dbg_y1, y[:, :]), 16)
            nc.sync.wait_ge(s_out.h, s_out.n)
            nc.all_engine_barrier()

        ph_c = ExitStack()
        x1b = ph_c.enter_context(nc.sbuf_tensor("x1b", [128, NCH * FFS], BF16))
        hbuf = ph_c.enter_context(nc.sbuf_tensor("hbuf", [128, NHC * FFS], BF16))
        wgb = [ph_c.enter_context(nc.sbuf_tensor(f"wgb{i}", [128, NCH * 128], BF16))
               for i in range(2)]
        wub = [ph_c.enter_context(nc.sbuf_tensor(f"wub{i}", [128, NCH * 128], BF16))
               for i in range(2)]
        wdd = [ph_c.enter_context(nc.sbuf_tensor(f"wdd{i}", [128, NHC * 128], BF16))
               for i in range(2)]
        y2_slice_done = [0] * NQS
        dps_st = ExitStack()
        dps = [dps_st.enter_context(nc.psum_tensor(f"dps{i}", [128, 512], F32))
               for i in range(1)]
        sgb = ph_c.enter_context(nc.sbuf_tensor("sgb", [128, 512], BF16))

        def load_gu(hc, wi):
            s_gu[wi].inc(nc.gpsimd.dma_start(
                wgb[wi][:, :].rearrange("p (c n) -> p c n", c=NCH),
                wgT.rearrange("(c p) n -> p c n", p=128)[:, :, hc * 128:(hc + 1) * 128]), 16)
            s_gu[wi].inc(nc.gpsimd.dma_start(
                wub[wi][:, :].rearrange("p (c n) -> p c n", c=NCH),
                wuT.rearrange("(c p) n -> p c n", p=128)[:, :, hc * 128:(hc + 1) * 128]), 16)
            return (wi, s_gu[wi].n)

        s_fpe = s_ffc
        s_fact = Sem(st, nc, "s_fact")
        s_fdve = Sem(st, nc, "s_fdve")
        wg_free = [0, 0]
        ln2_out_pending = []
        wdd_free = [0, 0]
        wdd_ready = [0, 0]
        s_wd2 = [s_wd, Sem(st, nc, "s_wdB")]

        def load_wd(dc, i):
            nc.gpsimd.wait_ge(s_fpe.h, wdd_free[i])
            s_wd2[i].inc(nc.gpsimd.dma_start(
                wdd[i][:, :].rearrange("p (hc n) -> p hc n", hc=NHC),
                wdT.rearrange("(hc p) n -> p hc n", p=128)[:, :, dc * 128:(dc + 1) * 128]), 16)
            wdd_ready[i] = s_wd2[i].n

        ln2 = ln1
        ln2.ready_of = lambda s: (s_fdve, y2_slice_done[s])

        def ln2_out(s):
            ln2.finish(s)
            nc.sync.wait_ge(s_lnV.h, ln2.apply_st[s % ln2.nst])
            s_out.inc(nc.sync.dma_start(
                yout.rearrange("(c p) t -> p c t", p=128)[:, :, s * 512:(s + 1) * 512],
                y[:, :].rearrange("p (c t) -> p c t", c=NCH)[:, :, s * 512:(s + 1) * 512]), 16)

        for fs in range(NFS):
            t0 = fs * FFS
            if fs == 0:
                nc.vector.wait_ge(s_prj.h, wo_pe_done)
                nc.vector.wait_ge(s_evV.h, wo_dve_done)
                nc.gpsimd.wait_ge(s_prj.h, wo_pe_done)
                nc.gpsimd.wait_ge(s_evV.h, wo_dve_done)
            hi = min((t0 + FFS) // 512 - 1, NQS - 1)
            nc.vector.wait_ge(s_lnV.h, ln1_done[hi])
            for m in range(NCH):
                s_fdve.inc(nc.vector.tensor_scalar_mul(
                    x1b[:, m * FFS:(m + 1) * FFS],
                    y[:, m * TOWN + t0: m * TOWN + t0 + FFS], cfg["QSCALE"]))
            x1b_ready = s_fdve.n
            if fs == 0 and NFS >= 2:
                ln1.finish_rows(NQS - 1)
            if fs == 0:
                preload_table_set(nc, "silu_and_others")
            pend = [(fs - 1) * NH2 + k for k in range(NH2)]
            if fs > 0:
                ln2.stats_act(pend[0])
            nc.tensor.wait_ge(s_fdve.h, x1b_ready)
            nc.gpsimd.wait_ge(s_fpe.h, wg_free[0])
            gu_ready = {0: load_gu(0, 0)}
            gps_free = 0
            ups_free = 0
            for hc in range(NHC):
                wi = hc % 2
                if fs == 0 and NFS >= 2 and hc == 4:
                    ln1.finish_tail(NQS - 1)
                    ln1_done[NQS - 1] = ln1.apply_st[(NQS - 1) % ln1.nst]
                if hc + 1 < NHC:
                    nwi = (hc + 1) % 2
                    nc.gpsimd.wait_ge(s_fpe.h, wg_free[nwi])
                    gu_ready[hc + 1] = load_gu(hc + 1, nwi)
                rwi, rv = gu_ready[hc]
                nc.tensor.wait_ge(s_gu[rwi].h, rv)
                for nh in range(NH2):
                    c0 = nh * 512
                    nc.tensor.wait_ge(s_fact.h, gps_free)
                    lastg = None
                    for m in range(NCH):
                        lastg = nc.tensor.matmul(
                            gps[:, :], wgb[wi][:, m * 128:(m + 1) * 128],
                            x1b[:, m * FFS + c0: m * FFS + c0 + 512],
                            start=(m == 0), stop=(m == NCH - 1))
                    s_fpe.inc(lastg)
                    g_done = s_fpe.n
                    nc.tensor.wait_ge(s_fdve.h, ups_free)
                    last = None
                    for m in range(NCH):
                        last = nc.tensor.matmul(
                            ups[:, :], wub[wi][:, m * 128:(m + 1) * 128],
                            x1b[:, m * FFS + c0: m * FFS + c0 + 512],
                            start=(m == 0), stop=(m == NCH - 1))
                    s_fpe.inc(last)
                    u_done = s_fpe.n
                    nc.scalar.wait_ge(s_fpe.h, g_done)
                    nc.scalar.wait_ge(s_fdve.h, ups_free)
                    s_fact.inc(nc.scalar.activation(sgb[:, :], gps[:, :], AF.Silu))
                    gps_free = s_fact.n
                    nc.vector.wait_ge(s_fpe.h, u_done)
                    nc.vector.wait_ge(s_fact.h, s_fact.n)
                    s_fdve.inc(nc.vector.tensor_mul(
                        hbuf[:, hc * FFS + c0: hc * FFS + c0 + 512], sgb[:, :], ups[:, :]))
                    ups_free = s_fdve.n
                wg_free[wi] = s_fpe.n
            nc.tensor.wait_ge(s_fdve.h, s_fdve.n)
            lastfs = (fs == NFS - 1)
            if lastfs:
                preload_table_set(nc, "natural_log_exp_and_others")
            if fs > 0:
                ln2.stats_pe(pend[0])
                if not lastfs:
                    ln2_out_pending.append(pend[0])
                for k in range(1, NH2):
                    ln2.stats_act(pend[k])
            load_wd(0, 0)
            load_wd(1, 1)
            dps_free = [0]
            for dc in range(NCH):
                wi2 = dc % 2
                if fs > 0 and lastfs and dc == 2:
                    ln2_out(pend[0])
                if fs > 0 and dc == NCH // 2:
                    for k in range(1, NH2):
                        ln2.stats_pe(pend[k])
                        if not lastfs:
                            ln2_out_pending.append(pend[k])
                if fs > 0 and lastfs and dc == NCH - 2:
                    for k in range(1, NH2):
                        ln2_out(pend[k])
                for nh in range(NH2):
                    c0 = nh * 512
                    i = 0
                    nc.tensor.wait_ge(s_fdve.h, dps_free[i])
                    if nh == 0:
                        nc.tensor.wait_ge(s_wd2[wi2].h, wdd_ready[wi2])
                    last = None
                    for hc in range(NHC):
                        last = nc.tensor.matmul(
                            dps[i][:, :], wdd[wi2][:, hc * 128: hc * 128 + 128],
                            hbuf[:, hc * FFS + c0: hc * FFS + c0 + 512],
                            start=(hc == 0), stop=(hc == NHC - 1))
                    s_fpe.inc(last)
                    if nh == NH2 - 1:
                        wdd_free[wi2] = s_fpe.n
                        if dc + 2 < NCH:
                            load_wd(dc + 2, wi2)
                    nc.vector.wait_ge(s_fpe.h, s_fpe.n)
                    yv = y[:, dc * TOWN + t0 + c0: dc * TOWN + t0 + c0 + 512]
                    s_fdve.inc(nc.vector.scalar_tensor_tensor(
                        yv, dps[i][:, :], cfg["QSCALE"], yv,
                        op0=ALU.mult, op1=ALU.add))
                    dps_free[i] = s_fdve.n
                    if dc == NCH - 1:
                        y2_slice_done[(t0 + c0) // 512] = s_fdve.n
        nc.tensor.wait_ge(s_fdve.h, s_fdve.n)
        nc.scalar.wait_ge(s_fdve.h, s_fdve.n)
        for sp_ in ln2_out_pending:
            ln2_out(sp_)
        dps_st.close()
        bc_x = ExitStack()
        bcx = [bc_x.enter_context(nc.psum_tensor(f"bcx{j}", [128, 512], F32))
               for j in range(2)]
        ln2.bc2.append(bcx)
        ln2.bc_done.append(0)
        ln2.apply_bc.append(0)
        ln2.nbc = 2
        fin = list(range((NFS - 1) * FFS // 512, NQS))
        for i, s in enumerate(fin):
            ln2.stats_act(s)
            ln2.stats_pe(s)
            if i > 0:
                ln2_out(fin[i - 1])
        ln2_out(fin[-1])
        nc.tensor.wait_ge(s_lnV.h, s_lnV.n)
        nc.scalar.wait_ge(s_lnV.h, s_lnV.n)
        bc_x.close()
        ln_ps.close()
        ffn_ps.close()
        ph_c.close()

        nc.sync.wait_ge(s_out.h, s_out.n)

    return nc




def make_core_inputs(cfg, x, wq, wk, wv, wo, wg, wu, wd):
    bf = ml_dtypes.bfloat16
    tril = np.triu(np.ones((128, 128), np.float32)).astype(bf)
    onesm = np.ones((128, 128), bf)
    zerosm = np.zeros((128, 128), bf)
    wqT = np.ascontiguousarray(wq.T)
    wkT = np.ascontiguousarray(wk.T)
    wvT = np.ascontiguousarray(wv.T)
    woT = np.ascontiguousarray(wo.T)
    wgT = np.ascontiguousarray(wg.T)
    wuT = np.ascontiguousarray(wu.T)
    wdT = np.ascontiguousarray(wd.T)
    ematv = np.zeros((2, 128), np.float32)
    ematv[0, 0:64] = 1.0
    ematv[1, 64:128] = 1.0
    in_maps = []
    for c in range(2 * cfg["B"]):
        b, H = c // 2, c % 2
        xf = np.ascontiguousarray(x[b].T)
        blocks = _own_blocks(cfg, H)
        cols = np.concatenate([np.arange(g * 128, (g + 1) * 128) for g in blocks])
        xqc = np.ascontiguousarray(xf[:, cols])
        in_maps.append(dict(
            xfm=xf, xq=xqc, wqT=wqT, wkT=wkT, wvT=wvT, woT=woT,
            wgT=wgT, wuT=wuT, wdT=wdT,
            maskA=(tril if H == 0 else onesm),
            maskB=(zerosm if H == 0 else tril),
            ematd=ematv))
    return in_maps


def assemble_output(cfg, results):
    T, D, B = cfg["T"], cfg["D"], cfg["B"]
    out = np.empty((B, T, D), np.float32)
    for c, r in enumerate(results):
        b, H = c // 2, c % 2
        blocks = _own_blocks(cfg, H)
        cols = np.concatenate([np.arange(g * 128, (g + 1) * 128) for g in blocks])
        out[b, cols, :] = np.asarray(r["yout"]).T
    return out


def kernel(L0_attn_x, L0_attn_Wq, L0_attn_Wk, L0_attn_Wv, L0_attn_Wo,
           L0_ffn_w_gate, L0_ffn_w_up, L0_ffn_w_down):
    cfg = FULL_CFG
    x = np.ascontiguousarray(np.asarray(L0_attn_x, dtype=np.float32))
    in_maps = make_core_inputs(
        cfg, x,
        np.asarray(L0_attn_Wq, np.float32), np.asarray(L0_attn_Wk, np.float32),
        np.asarray(L0_attn_Wv, np.float32), np.asarray(L0_attn_Wo, np.float32),
        np.asarray(L0_ffn_w_gate, np.int32), np.asarray(L0_ffn_w_up, np.int32),
        np.asarray(L0_ffn_w_down, np.int32))
    nc = build_program(cfg)
    res = run_bass_kernel_spmd(nc, in_maps, core_ids=list(range(2 * cfg["B"])))
    return assemble_output(cfg, res.results)


# version 24
# speedup vs baseline: 1.0360x; 1.0360x over previous
import numpy as np
import ml_dtypes
from contextlib import ExitStack

import concourse.bass as bass
import concourse.mybir as mybir
from concourse.bass_utils import run_bass_kernel_spmd

F32 = mybir.dt.float32
BF16 = mybir.dt.bfloat16
I32 = mybir.dt.int32
AF = mybir.ActivationFunctionType
AX = mybir.AxisListType
ALU = mybir.AluOpType

FULL_CFG = dict(D=1024, NH=16, DH=64, HID=3072, T=4096, B=4,
                SCALE=0.125, QSCALE=1.0 / 127.0, EPS=1e-5)


def _own_blocks(cfg, parity):
    return [2 * j + parity for j in range(cfg["T"] // 256)]


def preload_table_set(nc, set_name):
    from concourse.hw_specs import get_activation_tables
    names = list(get_activation_tables(nc.m.arch))
    idx = names.index(set_name)
    return nc.scalar.add_instruction(mybir.InstLoadActFuncSet(
        name=nc.get_next_instruction_name(), act_func_set_id=idx, ins=[], outs=[]))


class Sem:
    def __init__(self, stack, nc, name):
        self.h = stack.enter_context(nc.semaphore(name))
        self.n = 0

    def inc(self, instr, k=1):
        instr.then_inc(self.h, k)
        self.n += k
        return instr


def build_program(cfg):
    D, NH, DH, HID, T = cfg["D"], cfg["NH"], cfg["DH"], cfg["HID"], cfg["T"]
    NCH = D // 128
    NHC = HID // 128
    NB = T // 128
    TOWN = T // 2
    NBANK = TOWN // 512
    NKVS = T // 512
    NQS = TOWN // 512
    NPASS = 4
    HPP = NH // (2 * NPASS)
    PD = D // NPASS
    FFS = min(1024, TOWN)
    NFS = TOWN // FFS
    NH2 = FFS // 512
    VGW = 2 * HPP * 65

    nc = bass.Bass("TRN2", target_bir_lowering=False, num_devices=1)

    xfm = nc.dram_tensor("xfm", [D, T], F32, kind="ExternalInput").ap()
    xq = nc.dram_tensor("xq", [D, TOWN], F32, kind="ExternalInput").ap()
    wqT = nc.dram_tensor("wqT", [D, D], F32, kind="ExternalInput").ap()
    wkT = nc.dram_tensor("wkT", [D, D], F32, kind="ExternalInput").ap()
    wvT = nc.dram_tensor("wvT", [D, D], F32, kind="ExternalInput").ap()
    woT = nc.dram_tensor("woT", [D, D], F32, kind="ExternalInput").ap()
    wgT = nc.dram_tensor("wgT", [D, HID], I32, kind="ExternalInput").ap()
    wuT = nc.dram_tensor("wuT", [D, HID], I32, kind="ExternalInput").ap()
    wdT = nc.dram_tensor("wdT", [HID, D], I32, kind="ExternalInput").ap()
    maskA = nc.dram_tensor("maskA", [128, 128], BF16, kind="ExternalInput").ap()
    maskB = nc.dram_tensor("maskB", [128, 128], BF16, kind="ExternalInput").ap()
    ematd = nc.dram_tensor("ematd", [2, 128], F32, kind="ExternalInput").ap()
    yout = nc.dram_tensor("yout", [D, TOWN], F32, kind="ExternalOutput").ap()
    ldram = nc.dram_tensor("ldram", [NH, TOWN], F32, kind="Internal").ap()
    linvd = nc.dram_tensor("linvd", [NH, TOWN], F32, kind="Internal").ap()
    debug = cfg.get("debug", False)
    if debug:
        dbg_o = nc.dram_tensor("dbg_o", [128, NCH * TOWN], BF16, kind="ExternalOutput").ap()
        dbg_y1 = nc.dram_tensor("dbg_y1", [128, NCH * TOWN], F32, kind="ExternalOutput").ap()

    st = ExitStack()
    with st:
        consts = st.enter_context(nc.sbuf_tensor("consts", [128, 4], F32))
        ones_col_bf = st.enter_context(nc.sbuf_tensor("ones_col_bf", [128, 1], BF16))
        ones_row_f = st.enter_context(nc.sbuf_tensor("ones_row_f", [1, 128], F32))
        emat = st.enter_context(nc.sbuf_tensor("emat", [2, 128], F32))
        mka = st.enter_context(nc.sbuf_tensor("mka", [128, 128], BF16))
        mkb = st.enter_context(nc.sbuf_tensor("mkb", [128, 128], BF16))
        ofm_st = ExitStack()
        ofm = ofm_st.enter_context(nc.sbuf_tensor("ofm", [128, NCH * TOWN], BF16,
                                                  side="right"))

        s_xst = [Sem(st, nc, f"s_xst{i}") for i in range(2)]
        s_wld = [Sem(st, nc, f"s_wld{i}") for i in range(2)]
        s_wo = Sem(st, nc, "s_wo")
        s_ini = Sem(st, nc, "s_ini")
        s_prj = Sem(st, nc, "s_prj")
        s_evV = Sem(st, nc, "s_evV")
        s_s = Sem(st, nc, "s_s")
        s_exp = Sem(st, nc, "s_exp")
        s_msk = Sem(st, nc, "s_msk")
        s_pv = Sem(st, nc, "s_pv")
        s_oev = Sem(st, nc, "s_oev")
        s_ldm = [Sem(st, nc, f"s_ldm{i}") for i in range(4)]
        s_lpr = Sem(st, nc, "s_lpr")
        s_lwb = Sem(st, nc, "s_lwb")
        s_lp = [Sem(st, nc, f"s_lp{i}") for i in range(8)]
        s_cb = Sem(st, nc, "s_cb")
        s_nrm = Sem(st, nc, "s_nrm")
        s_res = [Sem(st, nc, f"s_res{i}") for i in range(4)]
        s_gu = [Sem(st, nc, f"s_gu{i}") for i in range(2)]
        s_wd = Sem(st, nc, "s_wd")
        s_ffc = Sem(st, nc, "s_ffc")
        s_out = Sem(st, nc, "s_out")
        s_lnA = Sem(st, nc, "s_lnA")
        s_lnV = Sem(st, nc, "s_lnV")
        s_lnP = Sem(st, nc, "s_lnP")

        nc.vector.memset(consts[:, 0:1], -1.0)
        nc.vector.memset(consts[:, 1:2], 0.5)
        nc.vector.memset(consts[:, 2:3], -0.5)
        nc.vector.memset(consts[:, 3:4], float(cfg["EPS"]))
        nc.const_aps.register(F32, -1.0, consts[:, 0:1])
        nc.const_aps.register(F32, 0.5, consts[:, 1:2])
        nc.const_aps.register(F32, -0.5, consts[:, 2:3])
        nc.const_aps.register(F32, float(cfg["EPS"]), consts[:, 3:4])
        nc.vector.memset(ones_col_bf[:, :], 1.0)
        nc.vector.memset(ones_row_f[:, :], 1.0)
        s_ini.inc(nc.gpsimd.dma_start(emat[:, :], ematd), 16)
        s_ini.inc(nc.gpsimd.dma_start(mka[:, :], maskA), 16)
        s_ini.inc(nc.gpsimd.dma_start(mkb[:, :], maskB), 16)
        nc.gpsimd.wait_ge(s_ini.h, s_ini.n)
        preload_table_set(nc, "natural_log_exp_and_others")

        attn_st = ExitStack()
        cbias = [attn_st.enter_context(nc.sbuf_tensor(f"cbias{i}", [128, 1], F32))
                 for i in range(2)]
        qsq2 = [attn_st.enter_context(nc.sbuf_tensor(f"qsq{i}", [128, 512], BF16))
                for i in range(2)]
        mrow = attn_st.enter_context(nc.sbuf_tensor("mrow", [1, 8], F32))
        lstg = [attn_st.enter_context(nc.sbuf_tensor(f"lstg{i}", [1, 512], F32))
                for i in range(4)]
        kbuf = [attn_st.enter_context(nc.sbuf_tensor(f"kbuf{i}", [128, HPP * T], BF16))
                for i in range(2)]
        vbuf = [attn_st.enter_context(nc.sbuf_tensor(f"vbuf{i}", [128, NB * VGW], BF16))
                for i in range(2)]
        qbuf = [attn_st.enter_context(nc.sbuf_tensor(f"qbuf{i}", [128, HPP * TOWN], BF16))
                for i in range(2)]
        xstg = [attn_st.enter_context(nc.sbuf_tensor(f"xstg{i}", [128, NCH * 512], BF16))
                for i in range(2)]
        wqb = [attn_st.enter_context(nc.sbuf_tensor(f"wqb{i}", [128, NCH * PD], BF16))
               for i in range(2)]
        wkb = [attn_st.enter_context(nc.sbuf_tensor(f"wkb{i}", [128, NCH * PD], BF16))
               for i in range(2)]
        wvb = [attn_st.enter_context(nc.sbuf_tensor(f"wvb{i}", [128, NCH * PD], BF16))
               for i in range(2)]
        pt = [attn_st.enter_context(nc.sbuf_tensor(f"pt{i}", [128, 1024], BF16))
              for i in range(2)]
        LRF = 4 * TOWN // 32
        lrecA = attn_st.enter_context(nc.sbuf_tensor("lrecA", [32, LRF], F32))
        lrecB = attn_st.enter_context(nc.sbuf_tensor("lrecB", [32, LRF], F32))
        lpair = [attn_st.enter_context(nc.sbuf_tensor(f"lpair{i}", [2, 512], F32))
                 for i in range(8)]
        for i in range(2):
            vt4 = vbuf[i][:, :].rearrange("p (jb h c) -> p jb h c", jb=NB, h=2 * HPP)
            nc.vector.memset(vt4[:, :, :, 64:65], 1.0)

        att_ps = ExitStack()
        sp = [att_ps.enter_context(nc.psum_tensor(f"sp{i}", [128, 1024], F32))
              for i in range(2)]
        ops = [att_ps.enter_context(nc.psum_tensor(f"op{i}", [65, 512], F32))
               for i in range(2)]
        pp = [att_ps.enter_context(nc.psum_tensor(f"pp{i}", [128, 512], F32))
              for i in range(2)]

        nc.all_engine_barrier()

        slot_free = [0, 0]
        pp_free = [0, 0]
        gidx = [0]

        def stage_x(src_ap, slot, t0):
            dst = xstg[slot][:, :].rearrange("p (c t) -> p c t", c=NCH)
            src = src_ap.rearrange("(c p) t -> p c t", p=128)[:, :, t0:t0 + 512]
            return nc.gpsimd.dma_start(dst, src)

        def load_w_quarter(dst, wsrc, p, sem):
            d = dst[:, :].rearrange("p (c n) -> p c n", c=NCH)
            s = wsrc.rearrange("(c p) n -> p c n", p=128)[:, :, p * PD:(p + 1) * PD]
            sem.inc(nc.gpsimd.dma_start(d, s), 16)

        def proj_gen(p):
            pb = p % 2
            load_w_quarter(wqb[pb], wqT, p, s_wld[pb])
            load_w_quarter(wkb[pb], wkT, p, s_wld[pb])
            load_w_quarter(wvb[pb], wvT, p, s_wld[pb])
            w_ready = s_wld[pb].n
            state = {"first": True}

            def group(emit_mm, emit_evac):
                pi = gidx[0] % 2
                gidx[0] += 1
                for m in range(NCH):
                    if m == 0:
                        nc.tensor.wait_ge(s_evV.h, pp_free[pi])
                        if state["first"]:
                            nc.tensor.wait_ge(s_wld[pb].h, w_ready)
                            state["first"] = False
                    last = emit_mm(pp[pi], m)
                    if m == NCH - 1:
                        s_prj.inc(last)
                        emit_evac(pp[pi], s_prj.n)
                        pp_free[pi] = s_evV.n
                    yield

            for ts in range(NKVS):
                slot = ts % 2
                nc.gpsimd.wait_ge(s_prj.h, slot_free[slot])
                s_xst[slot].inc(stage_x(xfm, slot, ts * 512), 16)
                x_ready = s_xst[slot].n
                nc.tensor.wait_ge(s_xst[slot].h, x_ready)
                xs = xstg[slot]
                for nk in range(HPP):
                    def mm_k(ps, m, nk=nk, xs=xs):
                        return nc.tensor.matmul(
                            ps[:, 0:512],
                            wkb[pb][:, m * PD + nk * 128: m * PD + nk * 128 + 128],
                            xs[:, m * 512:(m + 1) * 512],
                            start=(m == 0), stop=(m == NCH - 1))

                    def ev_k(ps, tgt, nk=nk, ts=ts):
                        nc.vector.wait_ge(s_prj.h, tgt)
                        s_evV.inc(nc.vector.tensor_copy(
                            kbuf[pb][:, nk * T + ts * 512: nk * T + ts * 512 + 512],
                            ps[:, 0:512]))
                    yield from group(mm_k, ev_k)
                for jb in range(4):
                    def mm_v(ps, m, jb=jb, xs=xs):
                        return nc.tensor.matmul(
                            ps[:, 0:PD],
                            xs[:, m * 512 + jb * 128: m * 512 + jb * 128 + 128],
                            wvb[pb][:, m * PD:(m + 1) * PD],
                            start=(m == 0), stop=(m == NCH - 1))

                    def ev_v(ps, tgt, jb=jb, ts=ts):
                        nc.vector.wait_ge(s_prj.h, tgt)
                        g0 = (ts * 4 + jb) * VGW
                        dst3 = vbuf[pb][:, g0:g0 + VGW].rearrange(
                            "p (h c) -> p h c", h=2 * HPP)[:, :, 0:64]
                        src3 = ps[:, 0:PD].rearrange("p (h c) -> p h c", h=2 * HPP)
                        s_evV.inc(nc.vector.tensor_copy(dst3, src3))
                    yield from group(mm_v, ev_v)
                slot_free[slot] = s_prj.n
            for qs in range(NQS):
                slot = qs % 2
                nc.gpsimd.wait_ge(s_prj.h, slot_free[slot])
                s_xst[slot].inc(stage_x(xq, slot, qs * 512), 16)
                x_ready = s_xst[slot].n
                nc.tensor.wait_ge(s_xst[slot].h, x_ready)
                xs = xstg[slot]
                for nq in range(HPP):
                    def mm_q(ps, m, nq=nq, xs=xs):
                        return nc.tensor.matmul(
                            ps[:, 0:512],
                            wqb[pb][:, m * PD + nq * 128: m * PD + nq * 128 + 128],
                            xs[:, m * 512:(m + 1) * 512],
                            start=(m == 0), stop=(m == NCH - 1))

                    def ev_q(ps, tgt, nq=nq, qs=qs):
                        nc.vector.wait_ge(s_prj.h, tgt)
                        s_evV.inc(nc.vector.tensor_scalar_mul(
                            qbuf[pb][:, nq * TOWN + qs * 512: nq * TOWN + qs * 512 + 512],
                            ps[:, 0:512], cfg["SCALE"]))
                    yield from group(mm_q, ev_q)
                slot_free[slot] = s_prj.n

        def c_compute(p):
            pb = p % 2
            cbp = pp[0][:, 0:1]
            nc.tensor.wait_ge(s_evV.h, s_evV.n)
            nc.scalar.wait_ge(s_evV.h, s_evV.n)
            nc.vector.wait_ge(s_nrm.h, s_nrm.n)
            nc.vector.wait_ge(s_lnP.h, s_lnP.n)
            s_lnV.inc(nc.vector.memset(mrow[:, 0:2], 0.0))

            srcs = ([(kbuf[pb][:, ck * T + ts * 512: ck * T + ts * 512 + 512], 0)
                     for ts in range(NKVS) for ck in range(HPP)] +
                    [(qbuf[pb][:, cq * TOWN + qs * 512: cq * TOWN + qs * 512 + 512], 1)
                     for qs in range(NQS) for cq in range(HPP)])
            mm_of = [0, 0]
            red_of = [0, 0]
            for i, (src_ap, slot) in enumerate(srcs):
                bi = i % 2
                nrmp = pp[bi][0:1, 0:512]
                nc.scalar.wait_ge(s_lnP.h, mm_of[bi])
                s_lnA.inc(nc.scalar.activation(qsq2[bi][:, :], src_ap, AF.Square))
                nc.tensor.wait_ge(s_lnA.h, s_lnA.n)
                nc.tensor.wait_ge(s_lnV.h, red_of[bi])
                s_lnP.inc(nc.tensor.matmul(nrmp, ones_col_bf[:, :], qsq2[bi][:, :],
                                           start=True, stop=True))
                mm_of[bi] = s_lnP.n
                nc.vector.wait_ge(s_lnP.h, mm_of[bi])
                s_lnV.inc(nc.vector.reduce_max(mrow[:, 2 + bi:3 + bi], nrmp, axis=AX.X))
                red_of[bi] = s_lnV.n
                nc.vector.wait_ge(s_lnV.h, s_lnV.n)
                s_lnV.inc(nc.vector.tensor_max(mrow[:, slot:slot + 1],
                                               mrow[:, slot:slot + 1],
                                               mrow[:, 2 + bi:3 + bi]))
            nc.scalar.wait_ge(s_lnV.h, s_lnV.n)
            s_lnA.inc(nc.scalar.activation(mrow[:, 4:5], mrow[:, 0:1], AF.Ln))
            nc.scalar.wait_ge(s_lnA.h, s_lnA.n)
            s_lnA.inc(nc.scalar.activation(mrow[:, 5:6], mrow[:, 1:2], AF.Ln))
            nc.vector.wait_ge(s_lnA.h, s_lnA.n)
            s_lnV.inc(nc.vector.tensor_add(mrow[:, 4:5], mrow[:, 4:5], mrow[:, 5:6]))
            nc.scalar.wait_ge(s_lnV.h, s_lnV.n)
            s_lnA.inc(nc.scalar.activation(mrow[:, 6:7], mrow[:, 4:5], AF.Exp, scale=0.5))
            nc.vector.wait_ge(s_lnA.h, s_lnA.n)
            s_lnV.inc(nc.vector.tensor_scalar_mul(mrow[:, 6:7], mrow[:, 6:7], -1.0))
            nc.tensor.wait_ge(s_lnV.h, s_lnV.n)
            s_lnP.inc(nc.tensor.matmul(cbp, ones_row_f[:, :], mrow[:, 6:7],
                                       start=True, stop=True))
            nc.vector.wait_ge(s_lnP.h, s_lnP.n)
            s_cb.inc(nc.vector.tensor_copy(cbias[pb][:, :], cbp))
            nc.tensor.wait_ge(s_cb.h, s_cb.n)

        lp_slot = [0]

        def norm_gen(p):
            for i in range(4):
                nc.gpsimd.wait_ge(s_ldm[i].h, s_ldm[i].n)
            s_lpr.inc(nc.gpsimd.dma_start(
                lrecA[:, :], ldram[4 * p:4 * p + 4].rearrange("h (a f) -> (h a) f", f=LRF)), 16)
            lrd = s_lpr.n

            def gen():
                nc.vector.wait_ge(s_lpr.h, lrd)
                s_nrm.inc(nc.vector.reciprocal(lrecB[:, :], lrecA[:, :]))
                nc.gpsimd.wait_ge(s_nrm.h, s_nrm.n)
                s_lwb.inc(nc.gpsimd.dma_start(
                    linvd[4 * p:4 * p + 4].rearrange("h (a f) -> (h a) f", f=LRF),
                    lrecB[:, :]), 16)
                wb = s_lwb.n
                yield
                idxs = [(2 * p + g, s) for g in range(2) for s in range(NQS)]
                ldone = {}
                def issue(i):
                    gcx, s = idxs[i]
                    j = lp_slot[0]
                    lp_slot[0] = (j + 1) % 8
                    nc.gpsimd.wait_ge(s_lwb.h, wb)
                    nc.gpsimd.wait_ge(s_evV.h, ldone.get(("f", j), 0))
                    s_lp[j].inc(nc.gpsimd.dma_start(
                        lpair[j][:, :],
                        linvd[2 * gcx:2 * gcx + 2, s * 512:s * 512 + 512]), 16)
                    ldone[i] = (j, s_lp[j].n)
                for i in range(len(idxs)):
                    issue(i)
                yield
                for i, (gcx, s) in enumerate(idxs):
                    j, lv = ldone[i]
                    pi = gidx[0] % 2
                    gidx[0] += 1
                    nc.tensor.wait_ge(s_evV.h, pp_free[pi])
                    nc.tensor.wait_ge(s_lp[j].h, lv)
                    s_prj.inc(nc.tensor.matmul(pp[pi][:, :], emat[:, :], lpair[j][:, :],
                                               start=True, stop=True))
                    tgt = s_prj.n
                    nc.vector.wait_ge(s_prj.h, tgt)
                    q0 = s * 512
                    s_evV.inc(nc.vector.tensor_mul(
                        ofm[:, gcx * TOWN + q0: gcx * TOWN + q0 + 512],
                        ofm[:, gcx * TOWN + q0: gcx * TOWN + q0 + 512], pp[pi][:, :]))
                    pp_free[pi] = s_evV.n
                    ldone[("f", j)] = s_evV.n
                    yield
            return gen()

        def chaing(*gens):
            for g in gens:
                yield from g

        exp_of_buf = [0, 0]
        pv_of_buf = [0, 0]
        oev_last = [0]
        lslot = [0]

        def attention_pass(p, filler, rate):
            pb = p % 2
            first_exp = [True]

            def emit_s(hp, bank, kb):
                d = kb - 8 * bank
                js = d // 2 if d > 0 else 0
                buf = kb % 2
                qbase = bank * 512
                nc.tensor.wait_ge(s_exp.h, exp_of_buf[buf])
                nc.tensor.matmul(
                    sp[buf][:, js * 128: 512],
                    kbuf[pb][0:64, hp * T + kb * 128: hp * T + kb * 128 + 128],
                    qbuf[pb][0:64, hp * TOWN + qbase + js * 128: hp * TOWN + qbase + 512],
                    start=True, stop=True)
                s_s.inc(nc.tensor.matmul(
                    sp[buf][:, 512 + js * 128: 1024],
                    kbuf[pb][64:128, hp * T + kb * 128: hp * T + kb * 128 + 128],
                    qbuf[pb][64:128, hp * TOWN + qbase + js * 128: hp * TOWN + qbase + 512],
                    start=True, stop=True))
                nc.scalar.wait_ge(s_s.h, s_s.n)
                nc.scalar.wait_ge(s_pv.h, pv_of_buf[buf])
                if first_exp[0]:
                    nc.scalar.wait_ge(s_cb.h, s_cb.n)
                    first_exp[0] = False
                sview = sp[buf][:, :].rearrange("p (h q) -> p h q", h=2)[:, :, js * 128:512]
                pview = pt[buf][:, :].rearrange("p (h q) -> p h q", h=2)[:, :, js * 128:512]
                s_exp.inc(nc.scalar.activation(pview, sview, AF.Exp,
                                               bias=cbias[pb][:, 0:1]))
                exp_of_buf[buf] = s_exp.n
                if d >= 0:
                    mk = mka if d % 2 == 0 else mkb
                    nc.vector.wait_ge(s_exp.h, s_exp.n)
                    nc.vector.tensor_mul(
                        pt[buf][:, js * 128: js * 128 + 128],
                        pt[buf][:, js * 128: js * 128 + 128], mk[:, :])
                    s_msk.inc(nc.vector.tensor_mul(
                        pt[buf][:, 512 + js * 128: 512 + js * 128 + 128],
                        pt[buf][:, 512 + js * 128: 512 + js * 128 + 128], mk[:, :]))
                return (s_exp.n, s_msk.n if d >= 0 else None)

            def emit_pv(hp, bank, kb, deps, o_started):
                d = kb - 8 * bank
                js = d // 2 if d > 0 else 0
                buf = kb % 2
                nkb = 8 * bank + 8
                exp_n, msk_n = deps
                if msk_n is not None:
                    nc.tensor.wait_ge(s_msk.h, msk_n)
                else:
                    nc.tensor.wait_ge(s_exp.h, exp_n)
                if not o_started:
                    nc.tensor.wait_ge(s_oev.h, oev_last[0])
                lastpv = None
                for hl in range(2):
                    lastpv = nc.tensor.matmul(
                        ops[hl][:, js * 128: 512],
                        vbuf[pb][:, kb * VGW + (2 * hp + hl) * 65:
                                 kb * VGW + (2 * hp + hl) * 65 + 65],
                        pt[buf][:, hl * 512 + js * 128: hl * 512 + 512],
                        start=(not o_started), stop=(kb == nkb - 1),
                        skip_group_check=True)
                s_pv.inc(lastpv)
                pv_of_buf[buf] = s_pv.n

            def fill(n):
                nonlocal filler
                if filler is None:
                    return
                for _ in range(n):
                    if next(filler, StopIteration) is StopIteration:
                        filler = None
                        break

            for hp in range(HPP):
                gc = HPP * p + hp
                for bank in range(NBANK):
                    nkb = 8 * bank + 8
                    qbase = bank * 512
                    deps = emit_s(hp, bank, 0)
                    for kb in range(nkb):
                        if kb + 1 < nkb:
                            ndeps = emit_s(hp, bank, kb + 1)
                        fill(rate)
                        emit_pv(hp, bank, kb, deps, o_started=(kb > 0))
                        if kb + 1 < nkb:
                            deps = ndeps
                    nc.vector.wait_ge(s_pv.h, s_pv.n)
                    for hl in range(2):
                        s_oev.inc(nc.vector.tensor_copy(
                            ofm[hl * 64:(hl + 1) * 64,
                                gc * TOWN + qbase: gc * TOWN + qbase + 512],
                            ops[hl][0:64, :]))
                        ls = lslot[0]
                        nc.vector.wait_ge(s_ldm[ls].h, s_ldm[ls].n)
                        s_oev.inc(nc.vector.tensor_copy(lstg[ls][:, :], ops[hl][64:65, :]))
                        nc.sync.wait_ge(s_oev.h, s_oev.n)
                        g = 2 * gc + hl
                        s_ldm[ls].inc(nc.sync.dma_start(
                            ldram[g:g + 1, qbase:qbase + 512], lstg[ls][:, :]), 16)
                        lslot[0] = (ls + 1) % 4
                    oev_last[0] = s_oev.n
            return filler

        for _ in proj_gen(0):
            pass
        c_compute(0)
        for p in range(NPASS):
            parts = []
            if p + 1 < NPASS:
                parts.append(proj_gen(p + 1))
            if p >= 1:
                parts.append(norm_gen(p - 1))
            filler = chaing(*parts) if parts else None
            filler = attention_pass(p, filler, rate=4)
            if filler is not None:
                for _ in filler:
                    pass
            if p + 1 < NPASS:
                c_compute(p + 1)
        for _ in norm_gen(NPASS - 1):
            pass
        for i in range(4):
            nc.gpsimd.wait_ge(s_ldm[i].h, s_ldm[i].n)
        nc.tensor.wait_ge(s_evV.h, s_evV.n)
        att_ps.close()
        attn_st.close()
        nc.all_engine_barrier()
        if debug:
            s_out.inc(nc.sync.dma_start(dbg_o, ofm[:, :]), 16)
            nc.sync.wait_ge(s_out.h, s_out.n)
            nc.all_engine_barrier()

        y = st.enter_context(nc.sbuf_tensor("y", [128, NCH * TOWN], F32))
        ysq = [st.enter_context(nc.sbuf_tensor(f"ysq{i}", [128, 512], BF16))
               for i in range(8)]
        ybf = [st.enter_context(nc.sbuf_tensor(f"ybf{i}", [128, 512], BF16))
               for i in range(8)]
        rows = st.enter_context(nc.sbuf_tensor("rows", [1, 4 * 512], F32))
        ph_b = ExitStack()
        wob = ph_b.enter_context(nc.sbuf_tensor("wob", [128, NCH * D], BF16))
        xres = [ph_b.enter_context(nc.sbuf_tensor(f"xres{i}", [128, 512], F32))
                for i in range(4)]
        ffn_ps = ExitStack()
        gps = ffn_ps.enter_context(nc.psum_tensor("gps", [128, 512], F32))
        ups = ffn_ps.enter_context(nc.psum_tensor("ups", [128, 512], F32))

        s_woB = Sem(st, nc, "s_woB")
        s_wo.inc(nc.gpsimd.dma_start(
            wob[:, :].rearrange("p (c n) -> p c n", c=NCH)[:, :, 0:D // 2],
            woT.rearrange("(c p) n -> p c n", p=128)[:, :, 0:D // 2]), 16)
        s_woB.inc(nc.gpsimd.dma_start(
            wob[:, :].rearrange("p (c n) -> p c n", c=NCH)[:, :, D // 2:D],
            woT.rearrange("(c p) n -> p c n", p=128)[:, :, D // 2:D]), 16)
        wo_ready = s_wo.n
        woB_ready = s_woB.n

        class LN:

            NRING = 4

            def __init__(self, stp, bc2, ysq, ybf, rows, sA, sV, sP, ready_of=None):
                self.stp, self.bc2 = stp, bc2
                self.ysq, self.ybf, self.rows = ysq, ybf, rows
                self.sA, self.sV, self.sP = sA, sV, sP
                self.nst, self.nbc = len(stp), len(bc2)
                nr = len(ysq)
                self.ysq_free = [0] * nr
                self.sq_done = {}
                self.stats_done = [0] * self.nst
                self.bc_done = [0] * self.nbc
                self.apply_st = [0] * self.nst
                self.apply_bc = [0] * self.nbc
                self.rows_free = 0
                self.ready_of = ready_of

            def stats_act(self, s):
                sA, sP = self.sA, self.sP
                nr = len(self.ysq)
                q0 = s * 512
                if self.ready_of is not None:
                    sem, val = self.ready_of(s)
                    nc.scalar.wait_ge(sem.h, val)
                for m in range(NCH):
                    bi = (s * NCH + m) % nr
                    nc.scalar.wait_ge(sP.h, self.ysq_free[bi])
                    sA.inc(nc.scalar.activation(
                        self.ysq[bi][:, :], y[:, m * TOWN + q0: m * TOWN + q0 + 512],
                        AF.Square))
                    sA.inc(nc.scalar.copy(
                        self.ybf[bi][:, :], y[:, m * TOWN + q0: m * TOWN + q0 + 512]))
                    self.sq_done[(s, m)] = sA.n

            def stats_pe(self, s):
                sV, sP = self.sV, self.sP
                nr = len(self.ysq)
                sb = s % self.nst
                nc.tensor.wait_ge(sV.h, self.apply_st[sb])
                for m in range(NCH):
                    bi = (s * NCH + m) % nr
                    nc.tensor.wait_ge(self.sA.h, self.sq_done.pop((s, m)))
                    nc.tensor.matmul(self.stp[sb][0:1, :], ones_col_bf[:, :],
                                     self.ybf[bi][:, :],
                                     start=(m == 0), stop=(m == NCH - 1))
                    sP.inc(nc.tensor.matmul(
                        self.stp[sb][32:33, :], ones_col_bf[:, :], self.ysq[bi][:, :],
                        start=(m == 0), stop=(m == NCH - 1)))
                    self.ysq_free[bi] = sP.n
                self.stats_done[sb] = sP.n

            def stats(self, s):
                self.stats_act(s)
                self.stats_pe(s)

            def finish_rows(self, s):
                sA, sV, sP = self.sA, self.sV, self.sP
                sb = s % self.nst
                rows = self.rows
                nc.scalar.wait_ge(sP.h, self.stats_done[sb])
                nc.scalar.wait_ge(sP.h, self.rows_free)
                sA.inc(nc.scalar.activation(rows[:, 0:512], self.stp[sb][0:1, :],
                                            AF.Square, scale=1.0 / D))
                nc.vector.wait_ge(sA.h, sA.n)
                nc.vector.wait_ge(sP.h, self.stats_done[sb])
                sV.inc(nc.vector.scalar_tensor_tensor(
                    rows[:, 512:1024], self.stp[sb][32:33, :], 1.0 / D, rows[:, 0:512],
                    op0=ALU.mult, op1=ALU.subtract))
                nc.scalar.wait_ge(sV.h, sV.n)
                sA.inc(nc.scalar.activation(rows[:, 1024:1536], rows[:, 512:1024],
                                            AF.Ln, bias=float(cfg["EPS"])))
                nc.scalar.wait_ge(sA.h, sA.n)
                sA.inc(nc.scalar.activation(rows[:, 1024:1536], rows[:, 1024:1536],
                                            AF.Exp, scale=-0.5))
                nc.vector.wait_ge(sA.h, sA.n)
                sV.inc(nc.vector.scalar_tensor_tensor(
                    rows[:, 1536:2048], self.stp[sb][0:1, :], -1.0 / D,
                    rows[:, 1024:1536], op0=ALU.mult, op1=ALU.mult))
                self.rows_done = sV.n

            def finish_bcast(self, s):
                sA, sV, sP = self.sA, self.sV, self.sP
                sb = s % self.nst
                bb = s % self.nbc
                rows = self.rows
                nc.tensor.wait_ge(sV.h, self.rows_done)
                nc.tensor.wait_ge(sV.h, self.apply_bc[bb])
                nc.tensor.matmul(self.bc2[bb][0][:, :], ones_row_f[:, :],
                                 rows[:, 1024:1536], start=True, stop=True)
                sP.inc(nc.tensor.matmul(self.bc2[bb][1][:, :], ones_row_f[:, :],
                                        rows[:, 1536:2048], start=True, stop=True))
                self.bc_done[bb] = sP.n
                self.rows_free = sP.n

            def apply_m(self, s, m0, m1):
                sV, sP = self.sV, self.sP
                sb = s % self.nst
                bb = s % self.nbc
                q0 = s * 512
                nc.vector.wait_ge(sP.h, self.bc_done[bb])
                for m in range(m0, m1):
                    yv = y[:, m * TOWN + q0: m * TOWN + q0 + 512]
                    sV.inc(nc.vector.tensor_mul(yv, yv, self.bc2[bb][0][:, :]))
                    nc.vector.wait_ge(sV.h, sV.n)
                    sV.inc(nc.vector.tensor_add(yv, yv, self.bc2[bb][1][:, :]))
                if m1 == NCH:
                    self.apply_st[sb] = sV.n
                    self.apply_bc[bb] = sV.n

            def finish_tail(self, s):
                self.finish_bcast(s)
                self.apply_m(s, 0, NCH)

            def finish(self, s):
                self.finish_rows(s)
                self.finish_tail(s)

        ln_ps = ExitStack()
        stp = [ln_ps.enter_context(nc.psum_tensor(f"stp{i}", [33, 512], F32))
               for i in range(2)]
        bc2 = [[ln_ps.enter_context(nc.psum_tensor(f"bc2{i}_{j}", [128, 512], F32))
                for j in range(2)] for i in range(1)]
        wo_ps = ExitStack()
        wps = [wo_ps.enter_context(nc.psum_tensor(f"wps{i}", [128, 512], F32))
               for i in range(2)]
        y_slice_done = [0] * NQS
        ln1_done = [0] * NQS

        ln1 = LN(stp, bc2, ysq, ybf, rows, s_lnA, s_lnV, s_lnP,
                 ready_of=lambda s: (s_evV, y_slice_done[s]))
        wo_iters = [(ncx, s) for s in range(NQS) for ncx in range(NCH)]
        wps_free = [0, 0]
        xr_free = [0, 0, 0, 0]
        xr_done = [0, 0, 0, 0]

        def issue_xres(idx):
            ncx, s = wo_iters[idx]
            j = idx % 4
            sem = s_res[j]
            nc.sync.wait_ge(s_evV.h, xr_free[j])
            sem.inc(nc.sync.dma_start(
                xres[j][:, :],
                xq.rearrange("(c p) t -> p c t", p=128)[:, ncx, s * 512:s * 512 + 512]), 16)
            xr_done[j] = sem.n

        for idx in range(min(4, len(wo_iters))):
            issue_xres(idx)
        for idx, (ncx, s) in enumerate(wo_iters):
            q0 = s * 512
            i = idx % 2
            j = idx % 4
            nc.tensor.wait_ge(s_evV.h, wps_free[i])
            if idx == 0:
                nc.tensor.wait_ge(s_wo.h, wo_ready)
            if idx == NCH // 2:
                nc.tensor.wait_ge(s_woB.h, woB_ready)
            last = None
            for kcx in range(NCH):
                last = nc.tensor.matmul(
                    wps[i][:, :], wob[:, kcx * D + ncx * 128: kcx * D + ncx * 128 + 128],
                    ofm[:, kcx * TOWN + q0: kcx * TOWN + q0 + 512],
                    start=(kcx == 0), stop=(kcx == NCH - 1))
            s_prj.inc(last)
            nc.vector.wait_ge(s_prj.h, s_prj.n)
            nc.vector.wait_ge(s_res[j].h, xr_done[j])
            s_evV.inc(nc.vector.tensor_add(
                y[:, ncx * TOWN + q0: ncx * TOWN + q0 + 512], wps[i][:, :], xres[j][:, :]))
            wps_free[i] = s_evV.n
            xr_free[j] = s_evV.n
            if idx + 4 < len(wo_iters):
                issue_xres(idx + 4)
            if ncx == 0 and s >= 2:
                ln1.apply_m(s - 2, 4, NCH)
                ln1_done[s - 2] = ln1.apply_st[(s - 2) % ln1.nst]
            if ncx == 2 and s >= 1:
                ln1.stats_pe(s - 1)
            if ncx == 5 and s >= 1:
                ln1.finish_rows(s - 1)
            if ncx == 6 and s >= 1:
                ln1.finish_bcast(s - 1)
            if ncx == NCH - 1:
                y_slice_done[s] = s_evV.n
                if s >= 1:
                    ln1.apply_m(s - 1, 0, 4)
                ln1.stats_act(s)
        ln1.apply_m(NQS - 2, 4, NCH)
        ln1_done[NQS - 2] = ln1.apply_st[(NQS - 2) % ln1.nst]
        wo_pe_done = s_prj.n
        wo_dve_done = s_evV.n
        ln1.stats_pe(NQS - 1)
        if NFS < 2:
            ln1.finish(NQS - 1)
            ln1_done[NQS - 1] = ln1.apply_st[(NQS - 1) % ln1.nst]
        ofm_st.close()
        wo_ps.close()
        ph_b.close()
        if debug:
            nc.sync.wait_ge(s_lnV.h, s_lnV.n)
            s_out.inc(nc.sync.dma_start(dbg_y1, y[:, :]), 16)
            nc.sync.wait_ge(s_out.h, s_out.n)
            nc.all_engine_barrier()

        ph_c = ExitStack()
        x1b = ph_c.enter_context(nc.sbuf_tensor("x1b", [128, NCH * FFS], BF16))
        hbuf = ph_c.enter_context(nc.sbuf_tensor("hbuf", [128, NHC * FFS], BF16))
        wgb = [ph_c.enter_context(nc.sbuf_tensor(f"wgb{i}", [128, NCH * 128], BF16))
               for i in range(2)]
        wub = [ph_c.enter_context(nc.sbuf_tensor(f"wub{i}", [128, NCH * 128], BF16))
               for i in range(2)]
        wdd = [ph_c.enter_context(nc.sbuf_tensor(f"wdd{i}", [128, NHC * 128], BF16))
               for i in range(2)]
        y2_slice_done = [0] * NQS
        dps_st = ExitStack()
        dps = [dps_st.enter_context(nc.psum_tensor(f"dps{i}", [128, 512], F32))
               for i in range(1)]
        sgb = ph_c.enter_context(nc.sbuf_tensor("sgb", [128, 512], BF16))

        def load_gu(hc, wi):
            s_gu[wi].inc(nc.gpsimd.dma_start(
                wgb[wi][:, :].rearrange("p (c n) -> p c n", c=NCH),
                wgT.rearrange("(c p) n -> p c n", p=128)[:, :, hc * 128:(hc + 1) * 128]), 16)
            s_gu[wi].inc(nc.gpsimd.dma_start(
                wub[wi][:, :].rearrange("p (c n) -> p c n", c=NCH),
                wuT.rearrange("(c p) n -> p c n", p=128)[:, :, hc * 128:(hc + 1) * 128]), 16)
            return (wi, s_gu[wi].n)

        s_fpe = s_ffc
        s_fact = Sem(st, nc, "s_fact")
        s_fdve = Sem(st, nc, "s_fdve")
        wg_free = [0, 0]
        ln2_out_pending = []
        wdd_free = [0, 0]
        wdd_ready = [0, 0]
        s_wd2 = [s_wd, Sem(st, nc, "s_wdB")]

        def load_wd(dc, i):
            nc.gpsimd.wait_ge(s_fpe.h, wdd_free[i])
            s_wd2[i].inc(nc.gpsimd.dma_start(
                wdd[i][:, :].rearrange("p (hc n) -> p hc n", hc=NHC),
                wdT.rearrange("(hc p) n -> p hc n", p=128)[:, :, dc * 128:(dc + 1) * 128]), 16)
            wdd_ready[i] = s_wd2[i].n

        ln2 = ln1
        ln2.ready_of = lambda s: (s_fdve, y2_slice_done[s])

        def ln2_out(s):
            ln2.finish(s)
            nc.sync.wait_ge(s_lnV.h, ln2.apply_st[s % ln2.nst])
            s_out.inc(nc.sync.dma_start(
                yout.rearrange("(c p) t -> p c t", p=128)[:, :, s * 512:(s + 1) * 512],
                y[:, :].rearrange("p (c t) -> p c t", c=NCH)[:, :, s * 512:(s + 1) * 512]), 16)

        for fs in range(NFS):
            t0 = fs * FFS
            if fs == 0:
                nc.vector.wait_ge(s_prj.h, wo_pe_done)
                nc.vector.wait_ge(s_evV.h, wo_dve_done)
                nc.gpsimd.wait_ge(s_prj.h, wo_pe_done)
                nc.gpsimd.wait_ge(s_evV.h, wo_dve_done)
            hi = min((t0 + FFS) // 512 - 1, NQS - 1)
            nc.vector.wait_ge(s_lnV.h, ln1_done[hi])
            for m in range(NCH):
                s_fdve.inc(nc.vector.tensor_scalar_mul(
                    x1b[:, m * FFS:(m + 1) * FFS],
                    y[:, m * TOWN + t0: m * TOWN + t0 + FFS], cfg["QSCALE"]))
            x1b_ready = s_fdve.n
            if fs == 0 and NFS >= 2:
                ln1.finish_rows(NQS - 1)
            if fs == 0:
                preload_table_set(nc, "silu_and_others")
            pend = [(fs - 1) * NH2 + k for k in range(NH2)]
            if fs > 0:
                ln2.stats_act(pend[0])
            nc.tensor.wait_ge(s_fdve.h, x1b_ready)
            nc.gpsimd.wait_ge(s_fpe.h, wg_free[0])
            gu_ready = {0: load_gu(0, 0)}
            gps_free = 0
            ups_free = 0
            for hc in range(NHC):
                wi = hc % 2
                if fs == 0 and NFS >= 2 and hc == 4:
                    ln1.finish_tail(NQS - 1)
                    ln1_done[NQS - 1] = ln1.apply_st[(NQS - 1) % ln1.nst]
                if hc + 1 < NHC:
                    nwi = (hc + 1) % 2
                    nc.gpsimd.wait_ge(s_fpe.h, wg_free[nwi])
                    gu_ready[hc + 1] = load_gu(hc + 1, nwi)
                rwi, rv = gu_ready[hc]
                nc.tensor.wait_ge(s_gu[rwi].h, rv)
                for nh in range(NH2):
                    c0 = nh * 512
                    nc.tensor.wait_ge(s_fact.h, gps_free)
                    lastg = None
                    for m in range(NCH):
                        lastg = nc.tensor.matmul(
                            gps[:, :], wgb[wi][:, m * 128:(m + 1) * 128],
                            x1b[:, m * FFS + c0: m * FFS + c0 + 512],
                            start=(m == 0), stop=(m == NCH - 1))
                    s_fpe.inc(lastg)
                    g_done = s_fpe.n
                    nc.tensor.wait_ge(s_fdve.h, ups_free)
                    last = None
                    for m in range(NCH):
                        last = nc.tensor.matmul(
                            ups[:, :], wub[wi][:, m * 128:(m + 1) * 128],
                            x1b[:, m * FFS + c0: m * FFS + c0 + 512],
                            start=(m == 0), stop=(m == NCH - 1))
                    s_fpe.inc(last)
                    u_done = s_fpe.n
                    nc.scalar.wait_ge(s_fpe.h, g_done)
                    nc.scalar.wait_ge(s_fdve.h, ups_free)
                    s_fact.inc(nc.scalar.activation(sgb[:, :], gps[:, :], AF.Silu))
                    gps_free = s_fact.n
                    nc.vector.wait_ge(s_fpe.h, u_done)
                    nc.vector.wait_ge(s_fact.h, s_fact.n)
                    s_fdve.inc(nc.vector.tensor_mul(
                        hbuf[:, hc * FFS + c0: hc * FFS + c0 + 512], sgb[:, :], ups[:, :]))
                    ups_free = s_fdve.n
                wg_free[wi] = s_fpe.n
            nc.tensor.wait_ge(s_fdve.h, s_fdve.n)
            lastfs = (fs == NFS - 1)
            if lastfs:
                preload_table_set(nc, "natural_log_exp_and_others")
            if fs > 0:
                ln2.stats_pe(pend[0])
                if not lastfs:
                    ln2_out_pending.append(pend[0])
                for k in range(1, NH2):
                    ln2.stats_act(pend[k])
            load_wd(0, 0)
            load_wd(1, 1)
            dps_free = [0]
            for dc in range(NCH):
                wi2 = dc % 2
                if fs > 0 and lastfs and dc == 2:
                    ln2_out(pend[0])
                if fs > 0 and dc == NCH // 2:
                    for k in range(1, NH2):
                        ln2.stats_pe(pend[k])
                        if not lastfs:
                            ln2_out_pending.append(pend[k])
                if fs > 0 and lastfs and dc == NCH - 2:
                    for k in range(1, NH2):
                        ln2_out(pend[k])
                for nh in range(NH2):
                    c0 = nh * 512
                    i = 0
                    nc.tensor.wait_ge(s_fdve.h, dps_free[i])
                    if nh == 0:
                        nc.tensor.wait_ge(s_wd2[wi2].h, wdd_ready[wi2])
                    last = None
                    for hc in range(NHC):
                        last = nc.tensor.matmul(
                            dps[i][:, :], wdd[wi2][:, hc * 128: hc * 128 + 128],
                            hbuf[:, hc * FFS + c0: hc * FFS + c0 + 512],
                            start=(hc == 0), stop=(hc == NHC - 1))
                    s_fpe.inc(last)
                    if nh == NH2 - 1:
                        wdd_free[wi2] = s_fpe.n
                        if dc + 2 < NCH:
                            load_wd(dc + 2, wi2)
                    nc.vector.wait_ge(s_fpe.h, s_fpe.n)
                    yv = y[:, dc * TOWN + t0 + c0: dc * TOWN + t0 + c0 + 512]
                    s_fdve.inc(nc.vector.scalar_tensor_tensor(
                        yv, dps[i][:, :], cfg["QSCALE"], yv,
                        op0=ALU.mult, op1=ALU.add))
                    dps_free[i] = s_fdve.n
                    if dc == NCH - 1:
                        y2_slice_done[(t0 + c0) // 512] = s_fdve.n
        nc.tensor.wait_ge(s_fdve.h, s_fdve.n)
        nc.scalar.wait_ge(s_fdve.h, s_fdve.n)
        for sp_ in ln2_out_pending:
            ln2_out(sp_)
        dps_st.close()
        bc_x = ExitStack()
        bcx = [bc_x.enter_context(nc.psum_tensor(f"bcx{j}", [128, 512], F32))
               for j in range(2)]
        ln2.bc2.append(bcx)
        ln2.bc_done.append(0)
        ln2.apply_bc.append(0)
        ln2.nbc = 2
        fin = list(range((NFS - 1) * FFS // 512, NQS))
        for i, s in enumerate(fin):
            ln2.stats_act(s)
            ln2.stats_pe(s)
            if i > 0:
                ln2_out(fin[i - 1])
        ln2_out(fin[-1])
        nc.tensor.wait_ge(s_lnV.h, s_lnV.n)
        nc.scalar.wait_ge(s_lnV.h, s_lnV.n)
        bc_x.close()
        ln_ps.close()
        ffn_ps.close()
        ph_c.close()

        nc.sync.wait_ge(s_out.h, s_out.n)

    return nc




def make_core_inputs(cfg, x, wq, wk, wv, wo, wg, wu, wd):
    bf = ml_dtypes.bfloat16
    tril = np.triu(np.ones((128, 128), np.float32)).astype(bf)
    onesm = np.ones((128, 128), bf)
    zerosm = np.zeros((128, 128), bf)
    wqT = np.ascontiguousarray(wq.T)
    wkT = np.ascontiguousarray(wk.T)
    wvT = np.ascontiguousarray(wv.T)
    woT = np.ascontiguousarray(wo.T)
    wgT = np.ascontiguousarray(wg.T)
    wuT = np.ascontiguousarray(wu.T)
    wdT = np.ascontiguousarray(wd.T)
    ematv = np.zeros((2, 128), np.float32)
    ematv[0, 0:64] = 1.0
    ematv[1, 64:128] = 1.0
    in_maps = []
    for c in range(2 * cfg["B"]):
        b, H = c // 2, c % 2
        xf = np.ascontiguousarray(x[b].T)
        blocks = _own_blocks(cfg, H)
        cols = np.concatenate([np.arange(g * 128, (g + 1) * 128) for g in blocks])
        xqc = np.ascontiguousarray(xf[:, cols])
        in_maps.append(dict(
            xfm=xf, xq=xqc, wqT=wqT, wkT=wkT, wvT=wvT, woT=woT,
            wgT=wgT, wuT=wuT, wdT=wdT,
            maskA=(tril if H == 0 else onesm),
            maskB=(zerosm if H == 0 else tril),
            ematd=ematv))
    return in_maps


def assemble_output(cfg, results):
    T, D, B = cfg["T"], cfg["D"], cfg["B"]
    out = np.empty((B, T, D), np.float32)
    for c, r in enumerate(results):
        b, H = c // 2, c % 2
        blocks = _own_blocks(cfg, H)
        cols = np.concatenate([np.arange(g * 128, (g + 1) * 128) for g in blocks])
        out[b, cols, :] = np.asarray(r["yout"]).T
    return out


def kernel(L0_attn_x, L0_attn_Wq, L0_attn_Wk, L0_attn_Wv, L0_attn_Wo,
           L0_ffn_w_gate, L0_ffn_w_up, L0_ffn_w_down):
    cfg = FULL_CFG
    x = np.ascontiguousarray(np.asarray(L0_attn_x, dtype=np.float32))
    in_maps = make_core_inputs(
        cfg, x,
        np.asarray(L0_attn_Wq, np.float32), np.asarray(L0_attn_Wk, np.float32),
        np.asarray(L0_attn_Wv, np.float32), np.asarray(L0_attn_Wo, np.float32),
        np.asarray(L0_ffn_w_gate, np.int32), np.asarray(L0_ffn_w_up, np.int32),
        np.asarray(L0_ffn_w_down, np.int32))
    nc = build_program(cfg)
    res = run_bass_kernel_spmd(nc, in_maps, core_ids=list(range(2 * cfg["B"])))
    return assemble_output(cfg, res.results)


# version 25 (final)
# speedup vs baseline: 1.0511x; 1.0145x over previous
import numpy as np
import ml_dtypes
from contextlib import ExitStack

import concourse.bass as bass
import concourse.mybir as mybir
from concourse.bass_utils import run_bass_kernel_spmd

F32 = mybir.dt.float32
BF16 = mybir.dt.bfloat16
I32 = mybir.dt.int32
AF = mybir.ActivationFunctionType
AX = mybir.AxisListType
ALU = mybir.AluOpType

FULL_CFG = dict(D=1024, NH=16, DH=64, HID=3072, T=4096, B=4,
                SCALE=0.125, QSCALE=1.0 / 127.0, EPS=1e-5)


def _own_blocks(cfg, parity):
    return [2 * j + parity for j in range(cfg["T"] // 256)]


def preload_table_set(nc, set_name):
    from concourse.hw_specs import get_activation_tables
    names = list(get_activation_tables(nc.m.arch))
    idx = names.index(set_name)
    return nc.scalar.add_instruction(mybir.InstLoadActFuncSet(
        name=nc.get_next_instruction_name(), act_func_set_id=idx, ins=[], outs=[]))


class Sem:
    def __init__(self, stack, nc, name):
        self.h = stack.enter_context(nc.semaphore(name))
        self.n = 0

    def inc(self, instr, k=1):
        instr.then_inc(self.h, k)
        self.n += k
        return instr


def build_program(cfg):
    D, NH, DH, HID, T = cfg["D"], cfg["NH"], cfg["DH"], cfg["HID"], cfg["T"]
    NCH = D // 128
    NHC = HID // 128
    NB = T // 128
    TOWN = T // 2
    NBANK = TOWN // 512
    NKVS = T // 512
    NQS = TOWN // 512
    NPASS = 4
    HPP = NH // (2 * NPASS)
    PD = D // NPASS
    FFS = min(1024, TOWN)
    NFS = TOWN // FFS
    NH2 = FFS // 512
    VGW = 2 * HPP * 65

    nc = bass.Bass("TRN2", target_bir_lowering=False, num_devices=1)

    xfm = nc.dram_tensor("xfm", [D, T], F32, kind="ExternalInput").ap()
    xq = nc.dram_tensor("xq", [D, TOWN], F32, kind="ExternalInput").ap()
    wqT = nc.dram_tensor("wqT", [D, D], F32, kind="ExternalInput").ap()
    wkT = nc.dram_tensor("wkT", [D, D], F32, kind="ExternalInput").ap()
    wvT = nc.dram_tensor("wvT", [D, D], F32, kind="ExternalInput").ap()
    woT = nc.dram_tensor("woT", [D, D], F32, kind="ExternalInput").ap()
    wgT = nc.dram_tensor("wgT", [D, HID], I32, kind="ExternalInput").ap()
    wuT = nc.dram_tensor("wuT", [D, HID], I32, kind="ExternalInput").ap()
    wdT = nc.dram_tensor("wdT", [HID, D], I32, kind="ExternalInput").ap()
    maskA = nc.dram_tensor("maskA", [128, 128], BF16, kind="ExternalInput").ap()
    maskB = nc.dram_tensor("maskB", [128, 128], BF16, kind="ExternalInput").ap()
    ematd = nc.dram_tensor("ematd", [2, 128], F32, kind="ExternalInput").ap()
    yout = nc.dram_tensor("yout", [D, TOWN], F32, kind="ExternalOutput").ap()
    ldram = nc.dram_tensor("ldram", [NH, TOWN], F32, kind="Internal").ap()
    linvd = nc.dram_tensor("linvd", [NH, TOWN], F32, kind="Internal").ap()
    debug = cfg.get("debug", False)
    if debug:
        dbg_o = nc.dram_tensor("dbg_o", [128, NCH * TOWN], BF16, kind="ExternalOutput").ap()
        dbg_y1 = nc.dram_tensor("dbg_y1", [128, NCH * TOWN], F32, kind="ExternalOutput").ap()

    st = ExitStack()
    with st:
        consts = st.enter_context(nc.sbuf_tensor("consts", [128, 4], F32))
        ones_col_bf = st.enter_context(nc.sbuf_tensor("ones_col_bf", [128, 1], BF16))
        ones_row_f = st.enter_context(nc.sbuf_tensor("ones_row_f", [1, 128], F32))
        emat = st.enter_context(nc.sbuf_tensor("emat", [2, 128], F32))
        mka = st.enter_context(nc.sbuf_tensor("mka", [128, 128], BF16))
        mkb = st.enter_context(nc.sbuf_tensor("mkb", [128, 128], BF16))
        ofm_st = ExitStack()
        ofm = ofm_st.enter_context(nc.sbuf_tensor("ofm", [128, NCH * TOWN], BF16,
                                                  side="right"))

        s_xst = [Sem(st, nc, f"s_xst{i}") for i in range(2)]
        s_wld = [Sem(st, nc, f"s_wld{i}") for i in range(2)]
        s_wo = Sem(st, nc, "s_wo")
        s_ini = Sem(st, nc, "s_ini")
        s_prj = Sem(st, nc, "s_prj")
        s_evV = Sem(st, nc, "s_evV")
        s_s = Sem(st, nc, "s_s")
        s_exp = Sem(st, nc, "s_exp")
        s_msk = Sem(st, nc, "s_msk")
        s_pv = Sem(st, nc, "s_pv")
        s_oev = Sem(st, nc, "s_oev")
        s_ldm = [Sem(st, nc, f"s_ldm{i}") for i in range(4)]
        s_lpr = Sem(st, nc, "s_lpr")
        s_lwb = Sem(st, nc, "s_lwb")
        s_lp = [Sem(st, nc, f"s_lp{i}") for i in range(8)]
        s_cb = Sem(st, nc, "s_cb")
        s_nrm = Sem(st, nc, "s_nrm")
        s_res = [Sem(st, nc, f"s_res{i}") for i in range(4)]
        s_gu = [Sem(st, nc, f"s_gu{i}") for i in range(2)]
        s_wd = Sem(st, nc, "s_wd")
        s_ffc = Sem(st, nc, "s_ffc")
        s_out = Sem(st, nc, "s_out")
        s_lnA = Sem(st, nc, "s_lnA")
        s_lnV = Sem(st, nc, "s_lnV")
        s_lnP = Sem(st, nc, "s_lnP")

        nc.vector.memset(consts[:, 0:1], -1.0)
        nc.vector.memset(consts[:, 1:2], 0.5)
        nc.vector.memset(consts[:, 2:3], -0.5)
        nc.vector.memset(consts[:, 3:4], float(cfg["EPS"]))
        nc.const_aps.register(F32, -1.0, consts[:, 0:1])
        nc.const_aps.register(F32, 0.5, consts[:, 1:2])
        nc.const_aps.register(F32, -0.5, consts[:, 2:3])
        nc.const_aps.register(F32, float(cfg["EPS"]), consts[:, 3:4])
        nc.vector.memset(ones_col_bf[:, :], 1.0)
        nc.vector.memset(ones_row_f[:, :], 1.0)
        s_ini.inc(nc.gpsimd.dma_start(emat[:, :], ematd), 16)
        s_ini.inc(nc.gpsimd.dma_start(mka[:, :], maskA), 16)
        s_ini.inc(nc.gpsimd.dma_start(mkb[:, :], maskB), 16)
        nc.gpsimd.wait_ge(s_ini.h, s_ini.n)
        preload_table_set(nc, "natural_log_exp_and_others")

        attn_st = ExitStack()
        cbias = [attn_st.enter_context(nc.sbuf_tensor(f"cbias{i}", [128, 1], F32))
                 for i in range(2)]
        qsq2 = [attn_st.enter_context(nc.sbuf_tensor(f"qsq{i}", [128, 512], BF16))
                for i in range(2)]
        mrow = attn_st.enter_context(nc.sbuf_tensor("mrow", [1, 8], F32))
        lstg = [attn_st.enter_context(nc.sbuf_tensor(f"lstg{i}", [1, 512], F32))
                for i in range(4)]
        kbuf = [attn_st.enter_context(nc.sbuf_tensor(f"kbuf{i}", [128, HPP * T], BF16))
                for i in range(2)]
        vbuf = [attn_st.enter_context(nc.sbuf_tensor(f"vbuf{i}", [128, NB * VGW], BF16))
                for i in range(2)]
        qbuf = [attn_st.enter_context(nc.sbuf_tensor(f"qbuf{i}", [128, HPP * TOWN], BF16))
                for i in range(2)]
        xstg = [attn_st.enter_context(nc.sbuf_tensor(f"xstg{i}", [128, NCH * 512], BF16))
                for i in range(2)]
        wqb = [attn_st.enter_context(nc.sbuf_tensor(f"wqb{i}", [128, NCH * PD], BF16))
               for i in range(2)]
        wkb = [attn_st.enter_context(nc.sbuf_tensor(f"wkb{i}", [128, NCH * PD], BF16))
               for i in range(2)]
        wvb = [attn_st.enter_context(nc.sbuf_tensor(f"wvb{i}", [128, NCH * PD], BF16))
               for i in range(2)]
        pt = [attn_st.enter_context(nc.sbuf_tensor(f"pt{i}", [128, 1024], BF16))
              for i in range(2)]
        LRF = 4 * TOWN // 32
        lrecA = attn_st.enter_context(nc.sbuf_tensor("lrecA", [32, LRF], F32))
        lrecB = attn_st.enter_context(nc.sbuf_tensor("lrecB", [32, LRF], F32))
        lpair = [attn_st.enter_context(nc.sbuf_tensor(f"lpair{i}", [2, 512], F32))
                 for i in range(8)]
        for i in range(2):
            vt4 = vbuf[i][:, :].rearrange("p (jb h c) -> p jb h c", jb=NB, h=2 * HPP)
            nc.vector.memset(vt4[:, :, :, 64:65], 1.0)

        att_ps = ExitStack()
        sp = [att_ps.enter_context(nc.psum_tensor(f"sp{i}", [128, 1024], F32))
              for i in range(2)]
        ops = [att_ps.enter_context(nc.psum_tensor(f"op{i}", [65, 512], F32))
               for i in range(2)]
        pp = [att_ps.enter_context(nc.psum_tensor(f"pp{i}", [128, 512], F32))
              for i in range(2)]

        nc.all_engine_barrier()

        slot_free = [0, 0]
        pp_free = [0, 0]
        gidx = [0]

        def stage_x(src_ap, slot, t0):
            dst = xstg[slot][:, :].rearrange("p (c t) -> p c t", c=NCH)
            src = src_ap.rearrange("(c p) t -> p c t", p=128)[:, :, t0:t0 + 512]
            return nc.gpsimd.dma_start(dst, src)

        def load_w_quarter(dst, wsrc, p, sem):
            d = dst[:, :].rearrange("p (c n) -> p c n", c=NCH)
            s = wsrc.rearrange("(c p) n -> p c n", p=128)[:, :, p * PD:(p + 1) * PD]
            sem.inc(nc.gpsimd.dma_start(d, s), 16)

        def proj_gen(p):
            pb = p % 2
            load_w_quarter(wqb[pb], wqT, p, s_wld[pb])
            load_w_quarter(wkb[pb], wkT, p, s_wld[pb])
            load_w_quarter(wvb[pb], wvT, p, s_wld[pb])
            w_ready = s_wld[pb].n
            state = {"first": True}

            def group(emit_mm, emit_evac):
                pi = gidx[0] % 2
                gidx[0] += 1
                for m in range(NCH):
                    if m == 0:
                        nc.tensor.wait_ge(s_evV.h, pp_free[pi])
                        if state["first"]:
                            nc.tensor.wait_ge(s_wld[pb].h, w_ready)
                            state["first"] = False
                    last = emit_mm(pp[pi], m)
                    if m == NCH - 1:
                        s_prj.inc(last)
                        emit_evac(pp[pi], s_prj.n)
                        pp_free[pi] = s_evV.n
                    yield

            for ts in range(NKVS):
                slot = ts % 2
                nc.gpsimd.wait_ge(s_prj.h, slot_free[slot])
                s_xst[slot].inc(stage_x(xfm, slot, ts * 512), 16)
                x_ready = s_xst[slot].n
                nc.tensor.wait_ge(s_xst[slot].h, x_ready)
                xs = xstg[slot]
                for nk in range(HPP):
                    def mm_k(ps, m, nk=nk, xs=xs):
                        return nc.tensor.matmul(
                            ps[:, 0:512],
                            wkb[pb][:, m * PD + nk * 128: m * PD + nk * 128 + 128],
                            xs[:, m * 512:(m + 1) * 512],
                            start=(m == 0), stop=(m == NCH - 1))

                    def ev_k(ps, tgt, nk=nk, ts=ts):
                        nc.vector.wait_ge(s_prj.h, tgt)
                        s_evV.inc(nc.vector.tensor_copy(
                            kbuf[pb][:, nk * T + ts * 512: nk * T + ts * 512 + 512],
                            ps[:, 0:512]))
                    yield from group(mm_k, ev_k)
                for jb in range(4):
                    def mm_v(ps, m, jb=jb, xs=xs):
                        return nc.tensor.matmul(
                            ps[:, 0:PD],
                            xs[:, m * 512 + jb * 128: m * 512 + jb * 128 + 128],
                            wvb[pb][:, m * PD:(m + 1) * PD],
                            start=(m == 0), stop=(m == NCH - 1))

                    def ev_v(ps, tgt, jb=jb, ts=ts):
                        nc.vector.wait_ge(s_prj.h, tgt)
                        g0 = (ts * 4 + jb) * VGW
                        dst3 = vbuf[pb][:, g0:g0 + VGW].rearrange(
                            "p (h c) -> p h c", h=2 * HPP)[:, :, 0:64]
                        src3 = ps[:, 0:PD].rearrange("p (h c) -> p h c", h=2 * HPP)
                        s_evV.inc(nc.vector.tensor_copy(dst3, src3))
                    yield from group(mm_v, ev_v)
                slot_free[slot] = s_prj.n
            for qs in range(NQS):
                slot = qs % 2
                nc.gpsimd.wait_ge(s_prj.h, slot_free[slot])
                s_xst[slot].inc(stage_x(xq, slot, qs * 512), 16)
                x_ready = s_xst[slot].n
                nc.tensor.wait_ge(s_xst[slot].h, x_ready)
                xs = xstg[slot]
                for nq in range(HPP):
                    def mm_q(ps, m, nq=nq, xs=xs):
                        return nc.tensor.matmul(
                            ps[:, 0:512],
                            wqb[pb][:, m * PD + nq * 128: m * PD + nq * 128 + 128],
                            xs[:, m * 512:(m + 1) * 512],
                            start=(m == 0), stop=(m == NCH - 1))

                    def ev_q(ps, tgt, nq=nq, qs=qs):
                        nc.vector.wait_ge(s_prj.h, tgt)
                        s_evV.inc(nc.vector.tensor_scalar_mul(
                            qbuf[pb][:, nq * TOWN + qs * 512: nq * TOWN + qs * 512 + 512],
                            ps[:, 0:512], cfg["SCALE"]))
                    yield from group(mm_q, ev_q)
                slot_free[slot] = s_prj.n

        def c_compute(p):
            pb = p % 2
            cbp = pp[0][:, 0:1]
            nc.tensor.wait_ge(s_evV.h, s_evV.n)
            nc.scalar.wait_ge(s_evV.h, s_evV.n)
            nc.vector.wait_ge(s_nrm.h, s_nrm.n)
            nc.vector.wait_ge(s_lnP.h, s_lnP.n)
            s_lnV.inc(nc.vector.memset(mrow[:, 0:2], 0.0))

            srcs = ([(kbuf[pb][:, ck * T + ts * 512: ck * T + ts * 512 + 512], 0)
                     for ts in range(NKVS) for ck in range(HPP)] +
                    [(qbuf[pb][:, cq * TOWN + qs * 512: cq * TOWN + qs * 512 + 512], 1)
                     for qs in range(NQS) for cq in range(HPP)])
            mm_of = [0, 0]
            red_of = [0, 0]
            for i, (src_ap, slot) in enumerate(srcs):
                bi = i % 2
                nrmp = pp[bi][0:1, 0:512]
                nc.scalar.wait_ge(s_lnP.h, mm_of[bi])
                s_lnA.inc(nc.scalar.activation(qsq2[bi][:, :], src_ap, AF.Square))
                nc.tensor.wait_ge(s_lnA.h, s_lnA.n)
                nc.tensor.wait_ge(s_lnV.h, red_of[bi])
                s_lnP.inc(nc.tensor.matmul(nrmp, ones_col_bf[:, :], qsq2[bi][:, :],
                                           start=True, stop=True))
                mm_of[bi] = s_lnP.n
                nc.vector.wait_ge(s_lnP.h, mm_of[bi])
                s_lnV.inc(nc.vector.reduce_max(mrow[:, 2 + bi:3 + bi], nrmp, axis=AX.X))
                red_of[bi] = s_lnV.n
                nc.vector.wait_ge(s_lnV.h, s_lnV.n)
                s_lnV.inc(nc.vector.tensor_max(mrow[:, slot:slot + 1],
                                               mrow[:, slot:slot + 1],
                                               mrow[:, 2 + bi:3 + bi]))
            nc.scalar.wait_ge(s_lnV.h, s_lnV.n)
            s_lnA.inc(nc.scalar.activation(mrow[:, 4:5], mrow[:, 0:1], AF.Ln))
            nc.scalar.wait_ge(s_lnA.h, s_lnA.n)
            s_lnA.inc(nc.scalar.activation(mrow[:, 5:6], mrow[:, 1:2], AF.Ln))
            nc.vector.wait_ge(s_lnA.h, s_lnA.n)
            s_lnV.inc(nc.vector.tensor_add(mrow[:, 4:5], mrow[:, 4:5], mrow[:, 5:6]))
            nc.scalar.wait_ge(s_lnV.h, s_lnV.n)
            s_lnA.inc(nc.scalar.activation(mrow[:, 6:7], mrow[:, 4:5], AF.Exp, scale=0.5))
            nc.vector.wait_ge(s_lnA.h, s_lnA.n)
            s_lnV.inc(nc.vector.tensor_scalar_mul(mrow[:, 6:7], mrow[:, 6:7], -1.0))
            nc.tensor.wait_ge(s_lnV.h, s_lnV.n)
            s_lnP.inc(nc.tensor.matmul(cbp, ones_row_f[:, :], mrow[:, 6:7],
                                       start=True, stop=True))
            nc.vector.wait_ge(s_lnP.h, s_lnP.n)
            s_cb.inc(nc.vector.tensor_copy(cbias[pb][:, :], cbp))
            nc.tensor.wait_ge(s_cb.h, s_cb.n)

        lp_slot = [0]

        def norm_gen(p):
            for i in range(4):
                nc.sync.wait_ge(s_ldm[i].h, s_ldm[i].n)
            s_lpr.inc(nc.sync.dma_start(
                lrecA[:, :], ldram[4 * p:4 * p + 4].rearrange("h (a f) -> (h a) f", f=LRF)), 16)
            lrd = s_lpr.n

            def gen():
                nc.vector.wait_ge(s_lpr.h, lrd)
                s_nrm.inc(nc.vector.reciprocal(lrecB[:, :], lrecA[:, :]))
                nc.sync.wait_ge(s_nrm.h, s_nrm.n)
                s_lwb.inc(nc.sync.dma_start(
                    linvd[4 * p:4 * p + 4].rearrange("h (a f) -> (h a) f", f=LRF),
                    lrecB[:, :]), 16)
                wb = s_lwb.n
                yield
                idxs = [(2 * p + g, s) for g in range(2) for s in range(NQS)]
                ldone = {}
                def issue(i):
                    gcx, s = idxs[i]
                    j = lp_slot[0]
                    lp_slot[0] = (j + 1) % 8
                    nc.sync.wait_ge(s_lwb.h, wb)
                    nc.sync.wait_ge(s_evV.h, ldone.get(("f", j), 0))
                    s_lp[j].inc(nc.sync.dma_start(
                        lpair[j][:, :],
                        linvd[2 * gcx:2 * gcx + 2, s * 512:s * 512 + 512]), 16)
                    ldone[i] = (j, s_lp[j].n)
                for i in range(len(idxs)):
                    issue(i)
                yield
                for i, (gcx, s) in enumerate(idxs):
                    j, lv = ldone[i]
                    pi = gidx[0] % 2
                    gidx[0] += 1
                    nc.tensor.wait_ge(s_evV.h, pp_free[pi])
                    nc.tensor.wait_ge(s_lp[j].h, lv)
                    s_prj.inc(nc.tensor.matmul(pp[pi][:, :], emat[:, :], lpair[j][:, :],
                                               start=True, stop=True))
                    tgt = s_prj.n
                    nc.vector.wait_ge(s_prj.h, tgt)
                    q0 = s * 512
                    s_evV.inc(nc.vector.tensor_mul(
                        ofm[:, gcx * TOWN + q0: gcx * TOWN + q0 + 512],
                        ofm[:, gcx * TOWN + q0: gcx * TOWN + q0 + 512], pp[pi][:, :]))
                    pp_free[pi] = s_evV.n
                    ldone[("f", j)] = s_evV.n
                    yield
            return gen()

        def chaing(*gens):
            for g in gens:
                yield from g

        exp_of_buf = [0, 0]
        pv_of_buf = [0, 0]
        oev_last = [0]
        lslot = [0]

        def attention_pass(p, filler, rate):
            pb = p % 2
            first_exp = [True]

            def emit_s(hp, bank, kb):
                d = kb - 8 * bank
                js = d // 2 if d > 0 else 0
                buf = kb % 2
                qbase = bank * 512
                nc.tensor.wait_ge(s_exp.h, exp_of_buf[buf])
                nc.tensor.matmul(
                    sp[buf][:, js * 128: 512],
                    kbuf[pb][0:64, hp * T + kb * 128: hp * T + kb * 128 + 128],
                    qbuf[pb][0:64, hp * TOWN + qbase + js * 128: hp * TOWN + qbase + 512],
                    start=True, stop=True)
                s_s.inc(nc.tensor.matmul(
                    sp[buf][:, 512 + js * 128: 1024],
                    kbuf[pb][64:128, hp * T + kb * 128: hp * T + kb * 128 + 128],
                    qbuf[pb][64:128, hp * TOWN + qbase + js * 128: hp * TOWN + qbase + 512],
                    start=True, stop=True))
                nc.scalar.wait_ge(s_s.h, s_s.n)
                nc.scalar.wait_ge(s_pv.h, pv_of_buf[buf])
                if first_exp[0]:
                    nc.scalar.wait_ge(s_cb.h, s_cb.n)
                    first_exp[0] = False
                sview = sp[buf][:, :].rearrange("p (h q) -> p h q", h=2)[:, :, js * 128:512]
                pview = pt[buf][:, :].rearrange("p (h q) -> p h q", h=2)[:, :, js * 128:512]
                s_exp.inc(nc.scalar.activation(pview, sview, AF.Exp,
                                               bias=cbias[pb][:, 0:1]))
                exp_of_buf[buf] = s_exp.n
                if d >= 0:
                    mk = mka if d % 2 == 0 else mkb
                    nc.vector.wait_ge(s_exp.h, s_exp.n)
                    nc.vector.tensor_mul(
                        pt[buf][:, js * 128: js * 128 + 128],
                        pt[buf][:, js * 128: js * 128 + 128], mk[:, :])
                    s_msk.inc(nc.vector.tensor_mul(
                        pt[buf][:, 512 + js * 128: 512 + js * 128 + 128],
                        pt[buf][:, 512 + js * 128: 512 + js * 128 + 128], mk[:, :]))
                return (s_exp.n, s_msk.n if d >= 0 else None)

            def emit_pv(hp, bank, kb, deps, o_started):
                d = kb - 8 * bank
                js = d // 2 if d > 0 else 0
                buf = kb % 2
                nkb = 8 * bank + 8
                exp_n, msk_n = deps
                if msk_n is not None:
                    nc.tensor.wait_ge(s_msk.h, msk_n)
                else:
                    nc.tensor.wait_ge(s_exp.h, exp_n)
                if not o_started:
                    nc.tensor.wait_ge(s_oev.h, oev_last[0])
                lastpv = None
                for hl in range(2):
                    lastpv = nc.tensor.matmul(
                        ops[hl][:, js * 128: 512],
                        vbuf[pb][:, kb * VGW + (2 * hp + hl) * 65:
                                 kb * VGW + (2 * hp + hl) * 65 + 65],
                        pt[buf][:, hl * 512 + js * 128: hl * 512 + 512],
                        start=(not o_started), stop=(kb == nkb - 1),
                        skip_group_check=True)
                s_pv.inc(lastpv)
                pv_of_buf[buf] = s_pv.n

            def fill(n):
                nonlocal filler
                if filler is None:
                    return
                for _ in range(n):
                    if next(filler, StopIteration) is StopIteration:
                        filler = None
                        break

            for hp in range(HPP):
                gc = HPP * p + hp
                for bank in range(NBANK):
                    nkb = 8 * bank + 8
                    qbase = bank * 512
                    deps = emit_s(hp, bank, 0)
                    for kb in range(nkb):
                        if kb + 1 < nkb:
                            ndeps = emit_s(hp, bank, kb + 1)
                        fill(rate)
                        emit_pv(hp, bank, kb, deps, o_started=(kb > 0))
                        if kb + 1 < nkb:
                            deps = ndeps
                    nc.vector.wait_ge(s_pv.h, s_pv.n)
                    for hl in range(2):
                        s_oev.inc(nc.vector.tensor_copy(
                            ofm[hl * 64:(hl + 1) * 64,
                                gc * TOWN + qbase: gc * TOWN + qbase + 512],
                            ops[hl][0:64, :]))
                        ls = lslot[0]
                        nc.vector.wait_ge(s_ldm[ls].h, s_ldm[ls].n)
                        s_oev.inc(nc.vector.tensor_copy(lstg[ls][:, :], ops[hl][64:65, :]))
                        nc.sync.wait_ge(s_oev.h, s_oev.n)
                        g = 2 * gc + hl
                        s_ldm[ls].inc(nc.sync.dma_start(
                            ldram[g:g + 1, qbase:qbase + 512], lstg[ls][:, :]), 16)
                        lslot[0] = (ls + 1) % 4
                    oev_last[0] = s_oev.n
            return filler

        for _ in proj_gen(0):
            pass
        c_compute(0)
        for p in range(NPASS):
            parts = []
            if p + 1 < NPASS:
                parts.append(proj_gen(p + 1))
            if p >= 1:
                parts.append(norm_gen(p - 1))
            filler = chaing(*parts) if parts else None
            filler = attention_pass(p, filler, rate=3)
            if filler is not None:
                for _ in filler:
                    pass
            if p + 1 < NPASS:
                c_compute(p + 1)
        for _ in norm_gen(NPASS - 1):
            pass
        for i in range(4):
            nc.gpsimd.wait_ge(s_ldm[i].h, s_ldm[i].n)
        nc.tensor.wait_ge(s_evV.h, s_evV.n)
        att_ps.close()
        attn_st.close()
        nc.all_engine_barrier()
        if debug:
            s_out.inc(nc.sync.dma_start(dbg_o, ofm[:, :]), 16)
            nc.sync.wait_ge(s_out.h, s_out.n)
            nc.all_engine_barrier()

        y = st.enter_context(nc.sbuf_tensor("y", [128, NCH * TOWN], F32))
        ysq = [st.enter_context(nc.sbuf_tensor(f"ysq{i}", [128, 512], BF16))
               for i in range(8)]
        ybf = [st.enter_context(nc.sbuf_tensor(f"ybf{i}", [128, 512], BF16))
               for i in range(8)]
        rows = st.enter_context(nc.sbuf_tensor("rows", [1, 4 * 512], F32))
        ph_b = ExitStack()
        wob = ph_b.enter_context(nc.sbuf_tensor("wob", [128, NCH * D], BF16))
        xres = [ph_b.enter_context(nc.sbuf_tensor(f"xres{i}", [128, 512], F32))
                for i in range(4)]
        ffn_ps = ExitStack()
        gps = ffn_ps.enter_context(nc.psum_tensor("gps", [128, 512], F32))
        ups = ffn_ps.enter_context(nc.psum_tensor("ups", [128, 512], F32))

        s_woB = Sem(st, nc, "s_woB")
        s_wo.inc(nc.gpsimd.dma_start(
            wob[:, :].rearrange("p (c n) -> p c n", c=NCH)[:, :, 0:D // 2],
            woT.rearrange("(c p) n -> p c n", p=128)[:, :, 0:D // 2]), 16)
        s_woB.inc(nc.gpsimd.dma_start(
            wob[:, :].rearrange("p (c n) -> p c n", c=NCH)[:, :, D // 2:D],
            woT.rearrange("(c p) n -> p c n", p=128)[:, :, D // 2:D]), 16)
        wo_ready = s_wo.n
        woB_ready = s_woB.n

        class LN:

            NRING = 4

            def __init__(self, stp, bc2, ysq, ybf, rows, sA, sV, sP, ready_of=None):
                self.stp, self.bc2 = stp, bc2
                self.ysq, self.ybf, self.rows = ysq, ybf, rows
                self.sA, self.sV, self.sP = sA, sV, sP
                self.nst, self.nbc = len(stp), len(bc2)
                nr = len(ysq)
                self.ysq_free = [0] * nr
                self.sq_done = {}
                self.stats_done = [0] * self.nst
                self.bc_done = [0] * self.nbc
                self.apply_st = [0] * self.nst
                self.apply_bc = [0] * self.nbc
                self.rows_free = 0
                self.ready_of = ready_of

            def stats_act(self, s):
                sA, sP = self.sA, self.sP
                nr = len(self.ysq)
                q0 = s * 512
                if self.ready_of is not None:
                    sem, val = self.ready_of(s)
                    nc.scalar.wait_ge(sem.h, val)
                for m in range(NCH):
                    bi = (s * NCH + m) % nr
                    nc.scalar.wait_ge(sP.h, self.ysq_free[bi])
                    sA.inc(nc.scalar.activation(
                        self.ysq[bi][:, :], y[:, m * TOWN + q0: m * TOWN + q0 + 512],
                        AF.Square))
                    sA.inc(nc.scalar.copy(
                        self.ybf[bi][:, :], y[:, m * TOWN + q0: m * TOWN + q0 + 512]))
                    self.sq_done[(s, m)] = sA.n

            def stats_pe(self, s):
                sV, sP = self.sV, self.sP
                nr = len(self.ysq)
                sb = s % self.nst
                nc.tensor.wait_ge(sV.h, self.apply_st[sb])
                for m in range(NCH):
                    bi = (s * NCH + m) % nr
                    nc.tensor.wait_ge(self.sA.h, self.sq_done.pop((s, m)))
                    nc.tensor.matmul(self.stp[sb][0:1, :], ones_col_bf[:, :],
                                     self.ybf[bi][:, :],
                                     start=(m == 0), stop=(m == NCH - 1))
                    sP.inc(nc.tensor.matmul(
                        self.stp[sb][32:33, :], ones_col_bf[:, :], self.ysq[bi][:, :],
                        start=(m == 0), stop=(m == NCH - 1)))
                    self.ysq_free[bi] = sP.n
                self.stats_done[sb] = sP.n

            def stats(self, s):
                self.stats_act(s)
                self.stats_pe(s)

            def finish_rows(self, s):
                sA, sV, sP = self.sA, self.sV, self.sP
                sb = s % self.nst
                rows = self.rows
                nc.scalar.wait_ge(sP.h, self.stats_done[sb])
                nc.scalar.wait_ge(sP.h, self.rows_free)
                sA.inc(nc.scalar.activation(rows[:, 0:512], self.stp[sb][0:1, :],
                                            AF.Square, scale=1.0 / D))
                nc.vector.wait_ge(sA.h, sA.n)
                nc.vector.wait_ge(sP.h, self.stats_done[sb])
                sV.inc(nc.vector.scalar_tensor_tensor(
                    rows[:, 512:1024], self.stp[sb][32:33, :], 1.0 / D, rows[:, 0:512],
                    op0=ALU.mult, op1=ALU.subtract))
                nc.scalar.wait_ge(sV.h, sV.n)
                sA.inc(nc.scalar.activation(rows[:, 1024:1536], rows[:, 512:1024],
                                            AF.Ln, bias=float(cfg["EPS"])))
                nc.scalar.wait_ge(sA.h, sA.n)
                sA.inc(nc.scalar.activation(rows[:, 1024:1536], rows[:, 1024:1536],
                                            AF.Exp, scale=-0.5))
                nc.vector.wait_ge(sA.h, sA.n)
                sV.inc(nc.vector.scalar_tensor_tensor(
                    rows[:, 1536:2048], self.stp[sb][0:1, :], -1.0 / D,
                    rows[:, 1024:1536], op0=ALU.mult, op1=ALU.mult))
                self.rows_done = sV.n

            def finish_bcast(self, s):
                sA, sV, sP = self.sA, self.sV, self.sP
                sb = s % self.nst
                bb = s % self.nbc
                rows = self.rows
                nc.tensor.wait_ge(sV.h, self.rows_done)
                nc.tensor.wait_ge(sV.h, self.apply_bc[bb])
                nc.tensor.matmul(self.bc2[bb][0][:, :], ones_row_f[:, :],
                                 rows[:, 1024:1536], start=True, stop=True)
                sP.inc(nc.tensor.matmul(self.bc2[bb][1][:, :], ones_row_f[:, :],
                                        rows[:, 1536:2048], start=True, stop=True))
                self.bc_done[bb] = sP.n
                self.rows_free = sP.n

            def apply_m(self, s, m0, m1):
                sV, sP = self.sV, self.sP
                sb = s % self.nst
                bb = s % self.nbc
                q0 = s * 512
                nc.vector.wait_ge(sP.h, self.bc_done[bb])
                for m in range(m0, m1):
                    yv = y[:, m * TOWN + q0: m * TOWN + q0 + 512]
                    sV.inc(nc.vector.tensor_mul(yv, yv, self.bc2[bb][0][:, :]))
                    nc.vector.wait_ge(sV.h, sV.n)
                    sV.inc(nc.vector.tensor_add(yv, yv, self.bc2[bb][1][:, :]))
                if m1 == NCH:
                    self.apply_st[sb] = sV.n
                    self.apply_bc[bb] = sV.n

            def finish_tail(self, s):
                self.finish_bcast(s)
                self.apply_m(s, 0, NCH)

            def finish(self, s):
                self.finish_rows(s)
                self.finish_tail(s)

        ln_ps = ExitStack()
        stp = [ln_ps.enter_context(nc.psum_tensor(f"stp{i}", [33, 512], F32))
               for i in range(2)]
        bc2 = [[ln_ps.enter_context(nc.psum_tensor(f"bc2{i}_{j}", [128, 512], F32))
                for j in range(2)] for i in range(1)]
        wo_ps = ExitStack()
        wps = [wo_ps.enter_context(nc.psum_tensor(f"wps{i}", [128, 512], F32))
               for i in range(2)]
        y_slice_done = [0] * NQS
        ln1_done = [0] * NQS

        ln1 = LN(stp, bc2, ysq, ybf, rows, s_lnA, s_lnV, s_lnP,
                 ready_of=lambda s: (s_evV, y_slice_done[s]))
        wo_iters = [(ncx, s) for s in range(NQS) for ncx in range(NCH)]
        wps_free = [0, 0]
        xr_free = [0, 0, 0, 0]
        xr_done = [0, 0, 0, 0]

        def issue_xres(idx):
            ncx, s = wo_iters[idx]
            j = idx % 4
            sem = s_res[j]
            nc.sync.wait_ge(s_evV.h, xr_free[j])
            sem.inc(nc.sync.dma_start(
                xres[j][:, :],
                xq.rearrange("(c p) t -> p c t", p=128)[:, ncx, s * 512:s * 512 + 512]), 16)
            xr_done[j] = sem.n

        for idx in range(min(4, len(wo_iters))):
            issue_xres(idx)
        for idx, (ncx, s) in enumerate(wo_iters):
            q0 = s * 512
            i = idx % 2
            j = idx % 4
            nc.tensor.wait_ge(s_evV.h, wps_free[i])
            if idx == 0:
                nc.tensor.wait_ge(s_wo.h, wo_ready)
            if idx == NCH // 2:
                nc.tensor.wait_ge(s_woB.h, woB_ready)
            last = None
            for kcx in range(NCH):
                last = nc.tensor.matmul(
                    wps[i][:, :], wob[:, kcx * D + ncx * 128: kcx * D + ncx * 128 + 128],
                    ofm[:, kcx * TOWN + q0: kcx * TOWN + q0 + 512],
                    start=(kcx == 0), stop=(kcx == NCH - 1))
            s_prj.inc(last)
            nc.vector.wait_ge(s_prj.h, s_prj.n)
            nc.vector.wait_ge(s_res[j].h, xr_done[j])
            s_evV.inc(nc.vector.tensor_add(
                y[:, ncx * TOWN + q0: ncx * TOWN + q0 + 512], wps[i][:, :], xres[j][:, :]))
            wps_free[i] = s_evV.n
            xr_free[j] = s_evV.n
            if idx + 4 < len(wo_iters):
                issue_xres(idx + 4)
            if ncx == 0 and s >= 2:
                ln1.apply_m(s - 2, 4, NCH)
                ln1_done[s - 2] = ln1.apply_st[(s - 2) % ln1.nst]
            if ncx == 2 and s >= 1:
                ln1.stats_pe(s - 1)
            if ncx == 5 and s >= 1:
                ln1.finish_rows(s - 1)
            if ncx == 6 and s >= 1:
                ln1.finish_bcast(s - 1)
            if ncx == NCH - 1:
                y_slice_done[s] = s_evV.n
                if s >= 1:
                    ln1.apply_m(s - 1, 0, 4)
                ln1.stats_act(s)
        ln1.apply_m(NQS - 2, 4, NCH)
        ln1_done[NQS - 2] = ln1.apply_st[(NQS - 2) % ln1.nst]
        wo_pe_done = s_prj.n
        wo_dve_done = s_evV.n
        ln1.stats_pe(NQS - 1)
        if NFS < 2:
            ln1.finish(NQS - 1)
            ln1_done[NQS - 1] = ln1.apply_st[(NQS - 1) % ln1.nst]
        ofm_st.close()
        wo_ps.close()
        ph_b.close()
        if debug:
            nc.sync.wait_ge(s_lnV.h, s_lnV.n)
            s_out.inc(nc.sync.dma_start(dbg_y1, y[:, :]), 16)
            nc.sync.wait_ge(s_out.h, s_out.n)
            nc.all_engine_barrier()

        ph_c = ExitStack()
        x1b = ph_c.enter_context(nc.sbuf_tensor("x1b", [128, NCH * FFS], BF16))
        hbuf = ph_c.enter_context(nc.sbuf_tensor("hbuf", [128, NHC * FFS], BF16))
        wgb = [ph_c.enter_context(nc.sbuf_tensor(f"wgb{i}", [128, NCH * 128], BF16))
               for i in range(2)]
        wub = [ph_c.enter_context(nc.sbuf_tensor(f"wub{i}", [128, NCH * 128], BF16))
               for i in range(2)]
        wdd = [ph_c.enter_context(nc.sbuf_tensor(f"wdd{i}", [128, NHC * 128], BF16))
               for i in range(2)]
        y2_slice_done = [0] * NQS
        dps_st = ExitStack()
        dps = [dps_st.enter_context(nc.psum_tensor(f"dps{i}", [128, 512], F32))
               for i in range(1)]
        sgb = ph_c.enter_context(nc.sbuf_tensor("sgb", [128, 512], BF16))

        def load_gu(hc, wi):
            s_gu[wi].inc(nc.gpsimd.dma_start(
                wgb[wi][:, :].rearrange("p (c n) -> p c n", c=NCH),
                wgT.rearrange("(c p) n -> p c n", p=128)[:, :, hc * 128:(hc + 1) * 128]), 16)
            s_gu[wi].inc(nc.gpsimd.dma_start(
                wub[wi][:, :].rearrange("p (c n) -> p c n", c=NCH),
                wuT.rearrange("(c p) n -> p c n", p=128)[:, :, hc * 128:(hc + 1) * 128]), 16)
            return (wi, s_gu[wi].n)

        s_fpe = s_ffc
        s_fact = Sem(st, nc, "s_fact")
        s_fdve = Sem(st, nc, "s_fdve")
        wg_free = [0, 0]
        ln2_out_pending = []
        wdd_free = [0, 0]
        wdd_ready = [0, 0]
        s_wd2 = [s_wd, Sem(st, nc, "s_wdB")]

        def load_wd(dc, i):
            nc.gpsimd.wait_ge(s_fpe.h, wdd_free[i])
            s_wd2[i].inc(nc.gpsimd.dma_start(
                wdd[i][:, :].rearrange("p (hc n) -> p hc n", hc=NHC),
                wdT.rearrange("(hc p) n -> p hc n", p=128)[:, :, dc * 128:(dc + 1) * 128]), 16)
            wdd_ready[i] = s_wd2[i].n

        ln2 = ln1
        ln2.ready_of = lambda s: (s_fdve, y2_slice_done[s])

        def ln2_out(s):
            ln2.finish(s)
            nc.sync.wait_ge(s_lnV.h, ln2.apply_st[s % ln2.nst])
            s_out.inc(nc.sync.dma_start(
                yout.rearrange("(c p) t -> p c t", p=128)[:, :, s * 512:(s + 1) * 512],
                y[:, :].rearrange("p (c t) -> p c t", c=NCH)[:, :, s * 512:(s + 1) * 512]), 16)

        for fs in range(NFS):
            t0 = fs * FFS
            if fs == 0:
                nc.vector.wait_ge(s_prj.h, wo_pe_done)
                nc.vector.wait_ge(s_evV.h, wo_dve_done)
                nc.gpsimd.wait_ge(s_prj.h, wo_pe_done)
                nc.gpsimd.wait_ge(s_evV.h, wo_dve_done)
            hi = min((t0 + FFS) // 512 - 1, NQS - 1)
            nc.vector.wait_ge(s_lnV.h, ln1_done[hi])
            for m in range(NCH):
                s_fdve.inc(nc.vector.tensor_scalar_mul(
                    x1b[:, m * FFS:(m + 1) * FFS],
                    y[:, m * TOWN + t0: m * TOWN + t0 + FFS], cfg["QSCALE"]))
            x1b_ready = s_fdve.n
            if fs == 0 and NFS >= 2:
                ln1.finish_rows(NQS - 1)
            if fs == 0:
                preload_table_set(nc, "silu_and_others")
            pend = [(fs - 1) * NH2 + k for k in range(NH2)]
            if fs > 0:
                ln2.stats_act(pend[0])
            nc.tensor.wait_ge(s_fdve.h, x1b_ready)
            nc.gpsimd.wait_ge(s_fpe.h, wg_free[0])
            gu_ready = {0: load_gu(0, 0)}
            gps_free = 0
            ups_free = 0
            for hc in range(NHC):
                wi = hc % 2
                if fs == 0 and NFS >= 2 and hc == 4:
                    ln1.finish_tail(NQS - 1)
                    ln1_done[NQS - 1] = ln1.apply_st[(NQS - 1) % ln1.nst]
                if hc + 1 < NHC:
                    nwi = (hc + 1) % 2
                    nc.gpsimd.wait_ge(s_fpe.h, wg_free[nwi])
                    gu_ready[hc + 1] = load_gu(hc + 1, nwi)
                rwi, rv = gu_ready[hc]
                nc.tensor.wait_ge(s_gu[rwi].h, rv)
                for nh in range(NH2):
                    c0 = nh * 512
                    nc.tensor.wait_ge(s_fact.h, gps_free)
                    lastg = None
                    for m in range(NCH):
                        lastg = nc.tensor.matmul(
                            gps[:, :], wgb[wi][:, m * 128:(m + 1) * 128],
                            x1b[:, m * FFS + c0: m * FFS + c0 + 512],
                            start=(m == 0), stop=(m == NCH - 1))
                    s_fpe.inc(lastg)
                    g_done = s_fpe.n
                    nc.tensor.wait_ge(s_fdve.h, ups_free)
                    last = None
                    for m in range(NCH):
                        last = nc.tensor.matmul(
                            ups[:, :], wub[wi][:, m * 128:(m + 1) * 128],
                            x1b[:, m * FFS + c0: m * FFS + c0 + 512],
                            start=(m == 0), stop=(m == NCH - 1))
                    s_fpe.inc(last)
                    u_done = s_fpe.n
                    nc.scalar.wait_ge(s_fpe.h, g_done)
                    nc.scalar.wait_ge(s_fdve.h, ups_free)
                    s_fact.inc(nc.scalar.activation(sgb[:, :], gps[:, :], AF.Silu))
                    gps_free = s_fact.n
                    nc.vector.wait_ge(s_fpe.h, u_done)
                    nc.vector.wait_ge(s_fact.h, s_fact.n)
                    s_fdve.inc(nc.vector.tensor_mul(
                        hbuf[:, hc * FFS + c0: hc * FFS + c0 + 512], sgb[:, :], ups[:, :]))
                    ups_free = s_fdve.n
                wg_free[wi] = s_fpe.n
            nc.tensor.wait_ge(s_fdve.h, s_fdve.n)
            lastfs = (fs == NFS - 1)
            if lastfs:
                preload_table_set(nc, "natural_log_exp_and_others")
            if fs > 0:
                ln2.stats_pe(pend[0])
                if not lastfs:
                    ln2_out_pending.append(pend[0])
                for k in range(1, NH2):
                    ln2.stats_act(pend[k])
            load_wd(0, 0)
            load_wd(1, 1)
            dps_free = [0]
            for dc in range(NCH):
                wi2 = dc % 2
                if fs > 0 and lastfs and dc == 2:
                    ln2_out(pend[0])
                if fs > 0 and dc == NCH // 2:
                    for k in range(1, NH2):
                        ln2.stats_pe(pend[k])
                        if not lastfs:
                            ln2_out_pending.append(pend[k])
                if fs > 0 and lastfs and dc == NCH - 2:
                    for k in range(1, NH2):
                        ln2_out(pend[k])
                for nh in range(NH2):
                    c0 = nh * 512
                    i = 0
                    nc.tensor.wait_ge(s_fdve.h, dps_free[i])
                    if nh == 0:
                        nc.tensor.wait_ge(s_wd2[wi2].h, wdd_ready[wi2])
                    last = None
                    for hc in range(NHC):
                        last = nc.tensor.matmul(
                            dps[i][:, :], wdd[wi2][:, hc * 128: hc * 128 + 128],
                            hbuf[:, hc * FFS + c0: hc * FFS + c0 + 512],
                            start=(hc == 0), stop=(hc == NHC - 1))
                    s_fpe.inc(last)
                    if nh == NH2 - 1:
                        wdd_free[wi2] = s_fpe.n
                        if dc + 2 < NCH:
                            load_wd(dc + 2, wi2)
                    nc.vector.wait_ge(s_fpe.h, s_fpe.n)
                    yv = y[:, dc * TOWN + t0 + c0: dc * TOWN + t0 + c0 + 512]
                    s_fdve.inc(nc.vector.scalar_tensor_tensor(
                        yv, dps[i][:, :], cfg["QSCALE"], yv,
                        op0=ALU.mult, op1=ALU.add))
                    dps_free[i] = s_fdve.n
                    if dc == NCH - 1:
                        y2_slice_done[(t0 + c0) // 512] = s_fdve.n
        nc.tensor.wait_ge(s_fdve.h, s_fdve.n)
        nc.scalar.wait_ge(s_fdve.h, s_fdve.n)
        for sp_ in ln2_out_pending:
            ln2_out(sp_)
        dps_st.close()
        bc_x = ExitStack()
        bcx = [bc_x.enter_context(nc.psum_tensor(f"bcx{j}", [128, 512], F32))
               for j in range(2)]
        ln2.bc2.append(bcx)
        ln2.bc_done.append(0)
        ln2.apply_bc.append(0)
        ln2.nbc = 2
        fin = list(range((NFS - 1) * FFS // 512, NQS))
        for i, s in enumerate(fin):
            ln2.stats_act(s)
            ln2.stats_pe(s)
            if i > 0:
                ln2_out(fin[i - 1])
        ln2_out(fin[-1])
        nc.tensor.wait_ge(s_lnV.h, s_lnV.n)
        nc.scalar.wait_ge(s_lnV.h, s_lnV.n)
        bc_x.close()
        ln_ps.close()
        ffn_ps.close()
        ph_c.close()

        nc.sync.wait_ge(s_out.h, s_out.n)

    return nc




def make_core_inputs(cfg, x, wq, wk, wv, wo, wg, wu, wd):
    bf = ml_dtypes.bfloat16
    tril = np.triu(np.ones((128, 128), np.float32)).astype(bf)
    onesm = np.ones((128, 128), bf)
    zerosm = np.zeros((128, 128), bf)
    wqT = np.ascontiguousarray(wq.T)
    wkT = np.ascontiguousarray(wk.T)
    wvT = np.ascontiguousarray(wv.T)
    woT = np.ascontiguousarray(wo.T)
    wgT = np.ascontiguousarray(wg.T)
    wuT = np.ascontiguousarray(wu.T)
    wdT = np.ascontiguousarray(wd.T)
    ematv = np.zeros((2, 128), np.float32)
    ematv[0, 0:64] = 1.0
    ematv[1, 64:128] = 1.0
    in_maps = []
    for c in range(2 * cfg["B"]):
        b, H = c // 2, c % 2
        xf = np.ascontiguousarray(x[b].T)
        blocks = _own_blocks(cfg, H)
        cols = np.concatenate([np.arange(g * 128, (g + 1) * 128) for g in blocks])
        xqc = np.ascontiguousarray(xf[:, cols])
        in_maps.append(dict(
            xfm=xf, xq=xqc, wqT=wqT, wkT=wkT, wvT=wvT, woT=woT,
            wgT=wgT, wuT=wuT, wdT=wdT,
            maskA=(tril if H == 0 else onesm),
            maskB=(zerosm if H == 0 else tril),
            ematd=ematv))
    return in_maps


def assemble_output(cfg, results):
    T, D, B = cfg["T"], cfg["D"], cfg["B"]
    out = np.empty((B, T, D), np.float32)
    for c, r in enumerate(results):
        b, H = c // 2, c % 2
        blocks = _own_blocks(cfg, H)
        cols = np.concatenate([np.arange(g * 128, (g + 1) * 128) for g in blocks])
        out[b, cols, :] = np.asarray(r["yout"]).T
    return out


def kernel(L0_attn_x, L0_attn_Wq, L0_attn_Wk, L0_attn_Wv, L0_attn_Wo,
           L0_ffn_w_gate, L0_ffn_w_up, L0_ffn_w_down):
    cfg = FULL_CFG
    x = np.ascontiguousarray(np.asarray(L0_attn_x, dtype=np.float32))
    in_maps = make_core_inputs(
        cfg, x,
        np.asarray(L0_attn_Wq, np.float32), np.asarray(L0_attn_Wk, np.float32),
        np.asarray(L0_attn_Wv, np.float32), np.asarray(L0_attn_Wo, np.float32),
        np.asarray(L0_ffn_w_gate, np.int32), np.asarray(L0_ffn_w_up, np.int32),
        np.asarray(L0_ffn_w_down, np.int32))
    nc = build_program(cfg)
    res = run_bass_kernel_spmd(nc, in_maps, core_ids=list(range(2 * cfg["B"])))
    return assemble_output(cfg, res.results)
